# version 05
# speedup vs baseline: 1.1183x; 1.0011x over previous
import numpy as np
from contextlib import ExitStack

import concourse.bass as bass
import concourse.mybir as mybir
from concourse.bass_utils import run_bass_kernel_spmd

F32 = mybir.dt.float32
BF16 = mybir.dt.bfloat16
AX = mybir.AxisListType.X
OP = mybir.AluOpType
AF = mybir.ActivationFunctionType

T = 2048
D = 1024
NTC = T // 128
NDC = D // 128
LN_EPS = 1e-5

ENGINES = ("tensor", "vector", "scalar", "sync", "gpsimd")


class Prog:

    def __init__(self, nc, ctx):
        self.nc = nc
        self.items = {e: [] for e in ENGINES}
        self.cnt = {}
        self.sem = {}
        for e in ENGINES:
            for kind in ("c", "d"):
                name = e + kind
                self.sem[name] = ctx.enter_context(nc.semaphore("s_" + name))
                self.cnt[name] = 0
        self.lastwait = {}
        self.dmapool = {}
        self.dmaidx = {}
        self.semctx = ctx

    @staticmethod
    def _flat(deps):
        out = []
        for d in deps:
            if d is None:
                continue
            if isinstance(d, list):
                out.extend(x for x in d if x is not None)
            else:
                out.append(d)
        return out

    def _wait(self, eng, deps):
        need = {}
        for (s, v) in self._flat(deps):
            if s == eng + "c":
                import os
                if eng == "tensor" or (self.cnt[s] - v >= 10 and
                                       not os.environ.get("SELFWAIT_ALL")):
                    continue
            if v > need.get(s, 0):
                need[s] = v
        for s, v in sorted(need.items()):
            if v > self.lastwait.get((eng, s), 0):
                self.lastwait[(eng, s)] = v
                sem = self.sem[s]
                self.items[eng].append(lambda e, sem=sem, v=v: e.wait_ge(sem, v))

    def op(self, eng, fn, deps=()):
        self._wait(eng, deps)
        s = eng + "c"
        self.cnt[s] += 1
        v = self.cnt[s]
        sem = self.sem[s]
        self.items[eng].append(lambda e, fn=fn, sem=sem: fn(e).then_inc(sem, 1))
        return (s, v)

    def dma(self, eng, out, in_, deps=()):
        import os as _os_dma
        K = (int(_os_dma.environ.get("GP_K", "2")) if eng == "gpsimd"
             else int(_os_dma.environ.get("DMA_POOL_K", "12")))
        pool = self.dmapool.setdefault(eng, [])
        if not pool:
            for i in range(K):
                name = f"{eng}q{i}"
                pool.append(name)
                self.sem[name] = self.semctx.enter_context(
                    self.nc.semaphore("s_" + name))
                self.cnt[name] = 0
        i = self.dmaidx.get(eng, 0)
        self.dmaidx[eng] = i + 1
        s = pool[i % K]
        gate = self.cnt[s]
        self._wait(eng, list(deps) + ([(s, gate)] if gate else []))
        self.cnt[s] += 16
        v = self.cnt[s]
        sem = self.sem[s]
        self.items[eng].append(
            lambda e, out=out, in_=in_, sem=sem:
                e.dma_start(out=out, in_=in_).then_inc(sem, 16))
        return (s, v)

    def finalize(self, block, final_deps):
        self._wait("gpsimd", final_deps)
        for e in ENGINES:
            items = self.items[e]

            def emit(proxy, items=items):
                for it in items:
                    it(proxy)
            getattr(block, e if e != "sync" else "sync")(emit)


def last(*tokens):
    out = []
    for t in tokens:
        if t is None:
            continue
        if isinstance(t, list):
            out.extend(t)
        else:
            out.append(t)
    return out



def host_constants():
    c = {}
    c["ident"] = np.eye(128, dtype=np.float32).astype(ml_bf16())
    i = np.arange(128)
    c["tril"] = (i[None, :] <= i[:, None]).astype(np.float32)
    c["triu"] = (i[None, :] >= i[:, None]).astype(np.float32)
    c["stril"] = (i[None, :] < i[:, None]).astype(np.float32)
    c["trilneg"] = np.where(i[None, :] <= i[:, None], 0.0,
                            -1e30).astype(np.float32)
    c["identf"] = np.eye(128, dtype=np.float32)
    bf = ml_bf16()
    def perm(pairs):
        m = np.zeros((128, 96), np.float32)
        for (k, j) in pairs:
            m[k, j] = 1.0
        return m.astype(bf)
    c["perm_qcp0"] = perm([(j, j) for j in range(64)])
    c["perm_qcp1"] = perm([(64 + j, j) for j in range(64)])
    c["perm_krp"] = perm([(ii, 64 + ii) for ii in range(16)]
                         + [(32 + ii, 80 + ii) for ii in range(16)])
    for hl in range(8):
        c[f"perm_qx1_{hl}"] = perm([(hl * 16 + ii, 64 + ii)
                                    for ii in range(16)])
        c[f"perm_qx2_{hl}"] = perm([(hl * 16 + ii, 80 + ii)
                                    for ii in range(16)])
    import jax
    import jax.numpy as jnp
    with jax.default_device(jax.devices("cpu")[0]):
        inv = 1.0 / 10000.0 ** (jnp.arange(16, dtype=jnp.float32) / 16)
        ang = jnp.arange(T, dtype=jnp.float32)[:, None] * inv[None, :]
        cos = np.asarray(jnp.cos(ang))
        sin = np.asarray(jnp.sin(ang))
    r = np.arange(128) % 16
    c["rope_cos"] = cos.T[r].astype(np.float32)
    c["rope_sin"] = sin.T[r].astype(np.float32)
    pq = np.arange(128)[:, None]
    fk = np.arange(512)[None, :]
    c["dmask_add"] = np.stack(
        [np.where(fk <= pq - 128 * j, 0.0, -1e30) for j in range(4)]
    ).astype(np.float32)
    pk = np.arange(128)[:, None]
    fq = np.arange(512)[None, :]
    c["dmask_mul"] = np.stack(
        [np.where(pk <= fq - 128 * j, 1.0, 0.0) for j in range(4)]
    ).astype(np.float32)
    return c


def ml_bf16():
    import ml_dtypes
    return ml_dtypes.bfloat16



class Builder:

    def __init__(self, nc, ctx, p: "Prog"):
        self.nc = nc
        self.ctx = ctx
        self.p = p
        self.psum = [ctx.enter_context(nc.psum_tensor(f"pb{i}", [128, 512], F32))
                     for i in range(6)]
        self.psum.extend(ctx.enter_context(nc.psum_tensor(f"pb{i}", [128, 1024], BF16))
                         for i in (6, 7))
        self.bank_w = [None] * 8
        self.bank_r = [[] for _ in range(8)]
        self.X = self.sb("X", [128, NTC * 1024], F32)
        self.xn_f = None
        self.ident = self.sb("ident", [128, 128], BF16)
        self.tril = self.sb("tril", [128, 128], F32)
        self.triu = self.sb("triu", [128, 128], F32)
        self.stril = self.sb("stril", [128, 128], F32)
        self.trilneg = self.sb("trilneg", [128, 128], F32)
        self.identf = self.sb("identf", [128, 128], F32)
        self.onescol = self.sb("onescol", [128, 1], BF16)
        self.onesrow = self.sb("onesrow", [1, 128], BF16)
        self.onesrowf = self.sb("onesrowf", [1, 128], F32)
        self.ln_scr = self.sb("ln_scr", [128, 1024], BF16)
        self.stat = self.sb("stat", [128, 256], F32)
        self.astat = self.sb("astat", [128, 16 * NTC], F32)
        self.xn_tm = self.sb("xn_tm", [128, 4 * 1024], BF16)
        self.ln_scr2 = self.sb("ln_scr2", [128, 16], F32)
        self.wring = self.sb("wring", [128, 2 * 1024], F32)
        self.wring_rel = [None] * 2
        self.wring_i = 0
        self.cast_flip = 0
        self.name_i = 0
        self.dbg = {}

    def dump(self, name, ap, deps):
        if name in self.dbg:
            return self.p.dma("scalar", self.dbg[name], ap,
                              deps=self.flushed(deps))
        return None

    def load_consts(self, cap):
        p = self.p
        t = None
        toks = []
        for name, dst in (("ident", self.ident), ("tril", self.tril),
                          ("triu", self.triu), ("stril", self.stril),
                          ("trilneg", self.trilneg), ("identf", self.identf)):
            toks.append(p.dma("sync", dst[:, :], cap[name]))
        toks.append(p.op("vector", lambda e: e.memset(
            self.onescol[:, :], 1.0)))
        toks.append(p.op("vector", lambda e: e.memset(
            self.onesrow[0:1, :], 1.0)))
        toks.append(p.op("vector", lambda e: e.memset(
            self.onesrowf[0:1, :], 1.0)))
        return toks

    def load_X(self, x_ap):
        xr = x_ap.rearrange("(c p) d -> p c d", p=128)
        Xr = self.X[:, :].rearrange("p (c d) -> p c d", d=D)
        t1 = self.p.dma("sync", Xr[:, 0:8], xr[:, 0:8])
        t2 = self.p.dma("sync", Xr[:, 8:16], xr[:, 8:16])
        return last(t1, t2)

    def flushed(self, deps):
        out = [self.p.op("vector", lambda e: e.memset(
                   self.ln_scr2[0:1, 0:1], 0.0), deps=deps),
               self.p.op("scalar", lambda e: e.memzero(
                   self.ln_scr2[0:1, 1:2]), deps=deps)]
        return out

    def store_X(self, out_ap, deps):
        fdeps = self.flushed(deps)
        outr = out_ap.rearrange("(c p) d -> p c d", p=128)
        Xr = self.X[:, :].rearrange("p (c d) -> p c d", d=D)
        t1 = self.p.dma("scalar", outr[:, 0:8], Xr[:, 0:8], deps=fdeps)
        t2 = self.p.dma("scalar", outr[:, 8:16], Xr[:, 8:16], deps=fdeps)
        return last(t1, t2)

    def sb(self, name, shape, dt=F32):
        return self.ctx.enter_context(self.nc.sbuf_tensor(name, shape, dt))

    def bwrite(self, i, deps=()):
        return self.psum[i], list(deps) + self.bank_r[i]

    def bwrote(self, i, tok):
        self.bank_w[i] = tok
        self.bank_r[i] = []

    def bread(self, i, deps=()):
        d = list(deps) + list(self.bank_r[i])
        if self.bank_w[i] is not None:
            d.append(self.bank_w[i])
        return self.psum[i], d

    def bdidread(self, i, tok):
        self.bank_r[i].append(tok)

    def bank(self, i, deps=()):
        return self.bwrite(i, deps)

    def layernorm(self, src_ap_fn, sink=None):
        p = self.p
        if sink is None:
            sink = self._sink_transpose_xnf
        ring_rel = [None, None, None, None]
        out = []
        for c in range(NTC):
            src, sdeps = src_ap_fn(c)
            col = lambda j: self.stat[:, c * 16 + j * 4: c * 16 + j * 4 + 1]
            ssq = self.astat[:, c * 16: c * 16 + 1]
            sm = col(1)
            rstd = col(2)
            negmr = col(3)
            t1 = p.op("scalar", lambda e, src=src, ssq=ssq: e.activation(
                self.ln_scr[:, :], src, AF.Square, accum_out=ssq), deps=sdeps)
            t2 = p.op("vector", lambda e, src=src, sm=sm: e.tensor_reduce(
                sm, src, AX, OP.add), deps=sdeps)
            t3 = p.op("vector", lambda e, sm=sm: e.tensor_scalar(
                sm, sm, 1.0 / D, None, OP.mult), deps=[t2])
            t4 = p.op("vector", lambda e, sm=sm, ssq=ssq, rstd=rstd: e.scalar_tensor_tensor(
                rstd, sm, -1.0, sm, OP.mult, OP.mult), deps=[t3, t1])
            t5 = p.op("vector", lambda e, ssq=ssq, rstd=rstd: e.scalar_tensor_tensor(
                rstd, ssq, 1.0 / D, rstd, OP.mult, OP.add), deps=[t4])
            t5b = p.op("vector", lambda e, rstd=rstd: e.tensor_scalar(
                rstd, rstd, LN_EPS, None, OP.add), deps=[t5])
            t6 = p.op("scalar", lambda e, rstd=rstd: e.activation(
                rstd, rstd, AF.Sqrt), deps=[t5b])
            t7 = p.op("vector", lambda e, rstd=rstd: e.reciprocal(
                rstd, rstd), deps=[t6])
            t8 = p.op("vector", lambda e, sm=sm, rstd=rstd, negmr=negmr:
                      e.scalar_tensor_tensor(negmr, sm, -1.0, rstd, OP.mult,
                                             OP.mult), deps=[t7, t3])
            slot = c % 4
            dst = self.xn_tm[:, slot * 1024:(slot + 1) * 1024]
            ddeps = [ring_rel[slot]] if ring_rel[slot] is not None else []
            tn = p.op("vector", lambda e, src=src, dst=dst, rstd=rstd, negmr=negmr:
                      e.tensor_scalar(dst, src, rstd, negmr, OP.mult, OP.add),
                      deps=[t7, t8] + list(sdeps) + list(ddeps))
            rel, toks = sink(c, dst, tn)
            ring_rel[slot] = rel
            out.extend(toks)
        return out

    def _sink_transpose_xnf(self, c, dst, tn):
        p = self.p
        toks = []
        for k in range(NDC):
            bi = 6 + (k % 2)
            pb, bd = self.bwrite(bi, deps=[tn])
            tt = p.op("tensor", lambda e, pb=pb, dst=dst, k=k: e.transpose(
                pb[:, 0:128], dst[:, k * 128:(k + 1) * 128], self.ident[:, :]),
                deps=bd)
            self.bwrote(bi, tt)
            _, rd = self.bread(bi)
            ev = p.op("vector", lambda e, pb=pb, k=k, c=c: e.tensor_copy(
                self.xn_f[:, k * T + c * 128: k * T + (c + 1) * 128],
                pb[:, 0:128]), deps=rd)
            self.bdidread(bi, ev)
            toks.append(ev)
        return toks[-1], toks


    def load_weight(self, dst_bf, hbm_ap, kchunks, cols, deps=(), col0=0,
                    cast_eng=None):
        p = self.p
        toks = []
        for k in range(kchunks):
            for c0 in range(0, cols, 1024):
                w = min(1024, cols - c0)
                i = self.wring_i % 2
                self.wring_i += 1
                slot = self.wring[:, i * 1024: i * 1024 + w]
                d = list(deps)
                if self.wring_rel[i] is not None:
                    d.append(self.wring_rel[i])
                td = p.dma("sync", slot, hbm_ap[k * 128:(k + 1) * 128,
                                                c0:c0 + w], deps=d)
                if cast_eng is None:
                    eng = ("vector", "scalar")[self.cast_flip % 2]
                    self.cast_flip += 1
                else:
                    eng = cast_eng
                dst = dst_bf[:, col0 + k * cols + c0: col0 + k * cols + c0 + w]
                if eng == "vector":
                    tc = p.op(eng, lambda e, dst=dst, slot=slot:
                              e.tensor_copy(dst, slot), deps=[td])
                else:
                    tc = p.op(eng, lambda e, dst=dst, slot=slot:
                              e.copy(dst, slot), deps=[td])
                self.wring_rel[i] = tc
                toks.append(tc)
        return toks

    def load_weight_dma(self, dst_bf, hbm_ap, kchunks, cols, deps=(), col0=0,
                        queue="gpsimd"):
        toks = []
        for k in range(kchunks):
            toks.append(self.p.dma(
                queue,
                dst_bf[:, col0 + k * cols: col0 + (k + 1) * cols],
                hbm_ap[k * 128:(k + 1) * 128, :],
                deps=list(deps)))
        return toks

    def l0_attn(self, wq_ap, wk_ap, wv_ap, wo_ap, prev_toks):
        p = self.p
        scope2 = ExitStack()
        self.xn_f = scope2.enter_context(
            self.nc.sbuf_tensor("l0xnf", [128, NDC * T], BF16))
        Of = scope2.enter_context(self.nc.sbuf_tensor("l0Of", [128, NDC * T], BF16))
        xnf_toks = self.layernorm(
            lambda c: (self.X[:, c * 1024:(c + 1) * 1024], prev_toks))
        scope = ExitStack()
        sbl = lambda n, sh, dt: scope.enter_context(self.nc.sbuf_tensor(n, sh, dt))
        wq_b = sbl("l0wq", [128, 8192], BF16)
        wk_b = sbl("l0wk", [128, 8192], BF16)
        wv_b = sbl("l0wv", [128, 8192], BF16)
        wtok = []
        for (dst, ap) in ((wq_b, wq_ap), (wk_b, wk_ap), (wv_b, wv_ap)):
            wtok += self.load_weight(dst, ap, NDC, 1024)
        Qf = sbl("l0Qf", [128, T], BF16)
        Kf = sbl("l0Kf", [128, T], BF16)
        Vt = sbl("l0Vt", [128, T], BF16)
        ATs = [sbl(f"l0AT{i}", [128, 128], BF16) for i in range(2)]
        AMs = [sbl(f"l0AM{i}", [128, 128], BF16) for i in range(2)]
        Ws = [sbl(f"l0W{i}", [128, 128], BF16) for i in range(2)]
        BPs = [sbl(f"l0BP{i}", [128, 128], BF16) for i in range(2)]
        Rs = sbl("l0R", [128, 128], BF16)
        ZTs = [sbl(f"l0ZT{i}", [128, 64], BF16) for i in range(2)]
        QVb = sbl("l0QV", [128, 64], BF16)
        Ktm = sbl("l0Ktm", [128, 128], BF16)
        Qtm = sbl("l0Qtm", [128, 128], BF16)
        Ssb = sbl("l0S", [128, 64], F32)
        Dsb = sbl("l0D", [128, 64], F32)
        Sbf = sbl("l0Sb", [128, 64], BF16)
        Dbf = sbl("l0Db", [128, 64], BF16)
        mz = []
        import os
        npairs = int(os.environ.get("L0_NPAIRS", "8"))
        nch = int(os.environ.get("L0_NCH", str(NTC)))

        def frontier():
            return [(s, self.p.cnt[s]) for s in ("tensorc", "vectorc", "scalarc")]

        res_toks = None
        prev_pair_end = xnf_toks + mz + wtok
        for pr in range(npairs):
            proj_deps = prev_pair_end
            mz = [p.op("vector", lambda e, t=t: e.memset(t[:, :], 0.0),
                       deps=prev_pair_end) for t in (Ssb, Dsb, Sbf, Dbf)]
            proj_deps = proj_deps + mz
            ptoks = []
            for (dstW, wb) in ((Qf, wq_b), (Kf, wk_b)):
                for tt in range(4):
                    bi = tt % 2
                    pb, bd = self.bwrite(bi, deps=proj_deps)
                    mm = None
                    for k in range(NDC):
                        mm = p.op("tensor", lambda e, pb=pb, wb=wb, k=k, tt=tt, pr=pr:
                                  e.matmul(pb[:, 0:512],
                                           lhsT=wb[:, k * 1024 + pr * 128:
                                                   k * 1024 + pr * 128 + 128],
                                           rhs=self.xn_f[:, k * T + tt * 512:
                                                         k * T + tt * 512 + 512],
                                           start=(k == 0), stop=(k == NDC - 1)),
                                  deps=bd if k == 0 else ())
                    self.bwrote(bi, mm)
                    _, rd = self.bread(bi)
                    ev = p.op("scalar", lambda e, pb=pb, dstW=dstW, tt=tt:
                              e.copy(dstW[:, tt * 512: tt * 512 + 512],
                                     pb[:, 0:512]), deps=rd)
                    self.bdidread(bi, ev)
                    ptoks.append(ev)
            for c in range(NTC):
                bi = 2 + (c % 2)
                pb, bd = self.bwrite(bi, deps=proj_deps)
                mm = None
                for k in range(NDC):
                    mm = p.op("tensor", lambda e, pb=pb, k=k, c=c, pr=pr:
                              e.matmul(pb[:, 0:128],
                                       lhsT=self.xn_f[:, k * T + c * 128:
                                                      k * T + c * 128 + 128],
                                       rhs=wv_b[:, k * 1024 + pr * 128:
                                                k * 1024 + pr * 128 + 128],
                                       start=(k == 0), stop=(k == NDC - 1)),
                              deps=bd if k == 0 else ())
                self.bwrote(bi, mm)
                _, rd = self.bread(bi)
                ev = p.op("vector", lambda e, pb=pb, c=c: e.tensor_copy(
                    Vt[:, c * 128:(c + 1) * 128], pb[:, 0:128]), deps=rd)
                self.bdidread(bi, ev)
                ptoks.append(ev)
            scan_state = {}
            for c in range(nch):
                tms = {}
                for (bi, src, dst) in ((6, Kf, Ktm), (7, Qf, Qtm)):
                    pb, bd = self.bwrite(bi, deps=ptoks)
                    tt = p.op("tensor", lambda e, pb=pb, src=src, c=c: e.transpose(
                        pb[:, 0:128], src[:, c * 128:(c + 1) * 128],
                        self.ident[:, :]), deps=bd)
                    self.bwrote(bi, tt)
                    _, rd = self.bread(bi)
                    war = scan_state.get(("tmwar", dst.name), [])
                    ev = p.op("vector", lambda e, pb=pb, dst=dst: e.tensor_copy(
                        dst[:, :], pb[:, 0:128]), deps=rd + war)
                    self.bdidread(bi, ev)
                    tms[dst.name] = ev
                scan_state[("tmwar", Ktm.name)] = []
                scan_state[("tmwar", Qtm.name)] = []
                for pi in range(2):
                    self._l0_head(pr, c, pi, Qf, Kf, Vt, Of, ATs, AMs, Ws, BPs,
                                  Rs, ZTs, QVb, Ktm, Qtm, Ssb, Dsb, Sbf, Dbf,
                                  ptoks, tms, scan_state)
            prev_pair_end = frontier()
        scope.close()
        wo_b = scope2.enter_context(self.nc.sbuf_tensor("l0wo", [128, 8192], BF16))
        wo_t = self.load_weight(wo_b, wo_ap, NDC, 1024, deps=prev_pair_end)
        res_toks = []
        fin = prev_pair_end + wo_t
        for c in range(NTC):
            ct = []
            for half in range(2):
                bi = c % 2 + (2 if half else 0)
                pb, bd = self.bwrite(bi, deps=fin)
                mm = None
                for k in range(NDC):
                    mm = p.op("tensor", lambda e, pb=pb, k=k, c=c, half=half:
                              e.matmul(pb[:, 0:512],
                                       lhsT=Of[:, k * T + c * 128:
                                               k * T + c * 128 + 128],
                                       rhs=wo_b[:, k * 1024 + half * 512:
                                                k * 1024 + half * 512 + 512],
                                       start=(k == 0), stop=(k == NDC - 1)),
                              deps=bd if k == 0 else ())
                self.bwrote(bi, mm)
                _, rd = self.bread(bi)
                up = p.op("vector", lambda e, pb=pb, c=c, half=half:
                          e.tensor_tensor(
                              self.X[:, c * 1024 + half * 512:
                                     c * 1024 + half * 512 + 512],
                              self.X[:, c * 1024 + half * 512:
                                     c * 1024 + half * 512 + 512],
                              pb[:, 0:512], OP.add), deps=rd)
                self.bdidread(bi, up)
                ct.append(up)
            res_toks.append(ct[-1])
        scope2.close()
        return res_toks

    def _l0_head(self, pr, c, pi, Qf, Kf, Vt, Of, ATs, AMs, Ws, BPs, Rs, ZTs,
                 QVb, Ktm, Qtm, Ssb, Dsb, Sbf, Dbf, ptoks, tms, st):
        p = self.p
        hs = pi * 64
        sc0 = 0
        bAT, bBT, bSM = pi, 2 + pi, 4 + pi
        qsl = Qf[hs:hs + 64, c * 128:(c + 1) * 128]
        ksl = Kf[hs:hs + 64, c * 128:(c + 1) * 128]
        vsl = Vt[:, c * 128 + hs: c * 128 + hs + 64]
        Ssl = Ssb[hs:hs + 64, sc0:sc0 + 64]
        Dsl = Dsb[hs:hs + 64, sc0:sc0 + 64]
        Sbsl = Sbf[hs:hs + 64, sc0:sc0 + 64]
        Dbsl = Dbf[hs:hs + 64, sc0:sc0 + 64]
        tp0 = (hs, 0)
        tph = (hs, hs)
        import os
        stages = int(os.environ.get("L0_STAGES", "7"))
        if stages < 1:
            return
        key = (pr, pi)
        prev = st.get(key, {})
        war = st.get(("war", pi), {})
        nwar = {}
        if c > 0:
            pb, bd = self.bwrite(bSM, deps=[prev["sbf"]])
            r_mm = p.op("tensor", lambda e, pb=pb: e.matmul(
                pb[hs:hs + 64, 0:128], lhsT=Sbsl, rhs=qsl, start=True,
                stop=True, tile_position=tph), deps=bd)
            self.bwrote(bSM, r_mm)
            _, rd = self.bread(bSM)
            r_ev = p.op("scalar", lambda e, pb=pb: e.copy(
                Rs[hs:hs + 64, 0:128], pb[hs:hs + 64, 0:128]),
                deps=rd + war.get("R", []))
            self.bdidread(bSM, r_ev)
        pb_at, bd = self.bwrite(bAT, deps=ptoks)
        at_mm = p.op("tensor", lambda e, pb=pb_at: e.matmul(
            pb[:, 0:128], lhsT=ksl, rhs=qsl, start=True, stop=True,
            tile_position=tp0), deps=bd)
        self.bwrote(bAT, at_mm)
        _, rd = self.bread(bAT)
        at_ev = p.op("scalar", lambda e, pb=pb_at: e.copy(
            ATs[pi][:, :], pb[:, 0:128]), deps=rd + war.get("AT", []))
        self.bdidread(bAT, at_ev)
        _, rd = self.bread(bAT)
        am_ev = p.op("vector", lambda e, pb=pb_at: e.tensor_tensor(
            AMs[pi][:, :], pb[:, 0:128], self.triu[:, :], OP.mult),
            deps=rd + war.get("AT", []))
        self.bdidread(bAT, am_ev)
        _, rd = self.bread(bAT)
        w_ev = p.op("vector", lambda e, pb=pb_at: e.tensor_tensor(
            Ws[pi][:, :], pb[:, 0:128], self.stril[:, :], OP.mult),
            deps=rd + war.get("W", []))
        self.bdidread(bAT, w_ev)
        if stages < 2:
            st[("war", pi)] = nwar
            return
        pb_bt, bd = self.bwrite(bBT, deps=[at_ev, am_ev])
        bt_mm = p.op("tensor", lambda e, pb=pb_bt: e.matmul(
            pb[:, 0:128], lhsT=AMs[pi][:, :], rhs=ATs[pi][:, :],
            start=True, stop=(c == 0)), deps=bd)
        if c > 0:
            bt_mm = p.op("tensor", lambda e, pb=pb_bt: e.matmul(
                pb[:, 0:128], lhsT=qsl, rhs=Rs[hs:hs + 64, 0:128],
                start=False, stop=True, tile_position=tp0), deps=[r_ev])
        self.bwrote(bBT, bt_mm)
        _, rd = self.bread(bBT)
        bp_ev = p.op("vector", lambda e, pb=pb_bt: e.tensor_tensor(
            BPs[pi][:, :], pb[:, 0:128], self.triu[:, :], OP.mult),
            deps=rd + war.get("BP", []))
        self.bdidread(bBT, bp_ev)
        if stages < 3:
            st[("war", pi)] = nwar
            return
        pb_sm, bd = self.bwrite(bSM, deps=[bp_ev])
        o_mm = p.op("tensor", lambda e, pb=pb_sm: e.matmul(
            pb[hs:hs + 64, 384:512], lhsT=vsl, rhs=BPs[pi][:, :],
            start=True, stop=(c == 0), tile_position=(0, hs)), deps=bd)
        if c > 0:
            o_mm = p.op("tensor", lambda e, pb=pb_sm: e.matmul(
                pb[hs:hs + 64, 384:512], lhsT=Dbsl, rhs=qsl,
                start=False, stop=True, tile_position=tph),
                deps=[prev["dbf"]])
        self.bwrote(bSM, o_mm)
        _, rd = self.bread(bSM)
        o_ev = p.op("scalar", lambda e, pb=pb_sm: e.copy(
            Of[hs:hs + 64, pr * T + c * 128: pr * T + (c + 1) * 128],
            pb[hs:hs + 64, 384:512]), deps=rd)
        self.bdidread(bSM, o_ev)
        nwar["AT"] = [bt_mm]
        nwar["BP"] = [o_mm]
        nwar["R"] = [bt_mm]
        if c == NTC - 1:
            st[("war", pi)] = nwar
            return
        if stages < 4:
            st[("war", pi)] = nwar
            return
        ktm_t = tms[Ktm.name]
        qtm_t = tms[Qtm.name]
        pb, bd = self.bwrite(bSM, deps=[w_ev, ktm_t])
        zt_mm = p.op("tensor", lambda e, pb=pb: e.matmul(
            pb[:, 128:192], lhsT=Ws[pi][:, :], rhs=Ktm[:, hs:hs + 64],
            start=True, stop=True), deps=bd)
        self.bwrote(bSM, zt_mm)
        _, rd = self.bread(bSM)
        zt_ev = p.op("scalar", lambda e, pb=pb: e.mul(
            ZTs[pi][:, :], pb[:, 128:192], -1.0), deps=rd + war.get("ZT", []))
        self.bdidread(bSM, zt_ev)
        if stages < 5:
            st[("war", pi)] = nwar
            return
        pb, bd = self.bwrite(bSM, deps=[ktm_t])
        kk_mm = p.op("tensor", lambda e, pb=pb: e.matmul(
            pb[hs:hs + 64, 192:256], lhsT=Ktm[:, hs:hs + 64],
            rhs=Ktm[:, hs:hs + 64], start=True, stop=True,
            tile_position=(0, hs)), deps=bd)
        self.bwrote(bSM, kk_mm)
        _, rd = self.bread(bSM)
        s_up = p.op("vector", lambda e, pb=pb: e.tensor_tensor(
            Ssl, Ssl, pb[hs:hs + 64, 192:256], OP.add),
            deps=rd + ([prev["sup"]] if "sup" in prev else []))
        self.bdidread(bSM, s_up)
        sbf_d = [s_up]
        if c > 0:
            sbf_d.append(r_mm)
        sbf_c = p.op("scalar", lambda e: e.copy(Sbsl, Ssl), deps=sbf_d)
        if stages < 6:
            st[("war", pi)] = nwar
            return
        pb, bd = self.bwrite(bSM, deps=[qtm_t])
        qv_mm = p.op("tensor", lambda e, pb=pb: e.matmul(
            pb[hs:hs + 64, 256:320], lhsT=Qtm[:, hs:hs + 64], rhs=vsl,
            start=True, stop=True, tile_position=(0, hs)), deps=bd)
        self.bwrote(bSM, qv_mm)
        _, rd = self.bread(bSM)
        qv_ev = p.op("scalar", lambda e, pb=pb: e.copy(
            QVb[hs:hs + 64, 0:64], pb[hs:hs + 64, 256:320]),
            deps=rd + war.get("QV", []))
        self.bdidread(bSM, qv_ev)
        if stages < 7:
            st[("war", pi)] = nwar
            return
        pb, bd = self.bwrite(bSM, deps=[zt_ev])
        ku_mm = p.op("tensor", lambda e, pb=pb: e.matmul(
            pb[hs:hs + 64, 320:384], lhsT=ZTs[pi][:, :], rhs=vsl,
            start=True, stop=False, tile_position=(0, hs)), deps=bd)
        m11_mm = p.op("tensor", lambda e, pb=pb: e.matmul(
            pb[hs:hs + 64, 320:384], lhsT=Sbsl, rhs=QVb[hs:hs + 64, 0:64],
            start=False, stop=True, tile_position=tph),
            deps=[sbf_c, qv_ev])
        self.bwrote(bSM, m11_mm)
        _, rd = self.bread(bSM)
        d_up = p.op("vector", lambda e, pb=pb: e.tensor_tensor(
            Dsl, Dsl, pb[hs:hs + 64, 320:384], OP.add),
            deps=rd + ([prev["dup"]] if "dup" in prev else []))
        self.bdidread(bSM, d_up)
        dbf_d = [d_up]
        if c > 0:
            dbf_d.append(o_mm)
        dbf_c = p.op("scalar", lambda e: e.copy(Dbsl, Dsl), deps=dbf_d)
        nwar["W"] = [zt_mm]
        nwar["ZT"] = [ku_mm]
        nwar["QV"] = [m11_mm]
        st[("war", pi)] = nwar
        st.setdefault(("tmwar", Ktm.name), []).extend([zt_mm, kk_mm])
        st.setdefault(("tmwar", Qtm.name), []).append(qv_mm)
        st[key] = {"sbf": sbf_c, "dbf": dbf_c, "sup": s_up, "dup": d_up}


    def l0_ffn(self, gate_ap, w1_ap, b1_ap, w2_ap, b2_ap, prev_toks):
        p = self.p
        with ExitStack() as scope:
            sbl = lambda n, sh, dt: scope.enter_context(
                self.nc.sbuf_tensor(n, sh, dt))
            self.xn_f = sbl("moexnf", [128, NDC * T], BF16)
            xnf_toks = self.layernorm(
                lambda c: (self.X[:, c * 1024:(c + 1) * 1024], prev_toks))
            gate_b = sbl("moegate", [128, NDC * 16], BF16)
            Gtm = sbl("moeG", [128, NTC * 16], F32)
            Gf = sbl("moeGf", [128, T], BF16)
            b2s = sbl("moeb2", [128, 1024], BF16)
            b1t = sbl("moeb1", [128, 16], F32)
            b1n = sbl("moeb1n", [16, 128], BF16)
            w1h = [sbl("moew1a", [128, 8 * 1024], BF16),
                   sbl("moew1b", [128, 8 * 1024], BF16)]
            w2h = [sbl("moew2a", [128, 8 * 1024], BF16),
                   sbl("moew2b", [128, 8 * 1024], BF16)]
            hsup = sbl("moeh", [128, 8 * 512], BF16)
            gsc = sbl("moegs", [128, NTC * 32], F32)

            gtok = self.load_weight(gate_b, gate_ap, NDC, 16, deps=xnf_toks)
            tb2 = p.dma("gpsimd", b2s[0:16, 0:1024], b2_ap, deps=xnf_toks)
            tb2c = tb2
            gw_toks = []
            gf_toks = []
            for c in range(NTC):
                bi = c % 2
                pb, bd = self.bwrite(bi, deps=xnf_toks + gtok)
                mm = None
                for k in range(NDC):
                    mm = p.op("tensor", lambda e, pb=pb, k=k, c=c: e.matmul(
                        pb[:, 0:16],
                        lhsT=self.xn_f[:, k * T + c * 128: k * T + c * 128 + 128],
                        rhs=gate_b[:, k * 16:(k + 1) * 16],
                        start=(k == 0), stop=(k == NDC - 1)),
                        deps=bd if k == 0 else ())
                self.bwrote(bi, mm)
                _, rd = self.bread(bi)
                msk = gsc[:, c * 32: c * 32 + 16]
                col = lambda j: gsc[:, c * 32 + 16 + j * 4: c * 32 + 17 + j * 4]
                ge = Gtm[:, c * 16:(c + 1) * 16]
                m1 = col(0)
                t0 = p.op("vector", lambda e, pb=pb, ge=ge: e.tensor_copy(
                    ge, pb[:, 0:16]), deps=rd)
                self.bdidread(bi, t0)
                t1 = p.op("vector", lambda e, ge=ge, m1=m1: e.tensor_reduce(
                    m1, ge, AX, OP.max), deps=[t0])
                t1b = p.op("vector", lambda e, m1=m1: e.tensor_scalar(
                    m1, m1, -1.0, None, OP.mult), deps=[t1])
                t2 = p.op("scalar", lambda e, ge=ge, m1=m1: e.activation(
                    ge, ge, AF.Exp, bias=m1), deps=[t1b, t0])
                mk = msk
                mx = col(1)
                tprev = t2
                for it in range(2):
                    ta = p.op("vector", lambda e, ge=ge, mx=mx: e.tensor_reduce(
                        mx, ge, AX, OP.max), deps=[tprev])
                    tb = p.op("vector", lambda e, ge=ge, mx=mx, mk=mk:
                              e.tensor_scalar(mk, ge, mx, None, OP.is_ge),
                              deps=[ta])
                    tc_ = p.op("vector", lambda e, ge=ge, mk=mk:
                               e.scalar_tensor_tensor(ge, mk, -2.0, ge,
                                                      OP.mult, OP.add),
                               deps=[tb])
                    tprev = tc_
                t3 = p.op("vector", lambda e, ge=ge, mx=mx: e.tensor_reduce(
                    mx, ge, AX, OP.max), deps=[tprev])
                mneg = msk
                t4 = p.op("vector", lambda e, ge=ge, mneg=mneg: e.tensor_scalar(
                    mneg, ge, 0.0, None, OP.is_lt), deps=[t3])
                t5 = p.op("vector", lambda e, ge=ge, mneg=mneg:
                          e.scalar_tensor_tensor(ge, mneg, 2.0, ge, OP.mult,
                                                 OP.add), deps=[t4])
                t6 = p.op("vector", lambda e, ge=ge, mk=mk, mx=mx:
                          e.tensor_scalar(mk, ge, mx, None, OP.is_ge),
                          deps=[t5])
                t7 = p.op("vector", lambda e, ge=ge, mk=mk: e.tensor_tensor(
                    ge, ge, mk, OP.mult), deps=[t6])
                sm = col(0)
                t8 = p.op("vector", lambda e, ge=ge, sm=sm: e.tensor_reduce(
                    sm, ge, AX, OP.add), deps=[t7])
                t9 = p.op("vector", lambda e, sm=sm: e.reciprocal(sm, sm),
                          deps=[t8])
                t10 = p.op("vector", lambda e, ge=ge, sm=sm: e.tensor_scalar(
                    ge, ge, sm, None, OP.mult), deps=[t9])
                gw_toks.append(t10)
                war = gf_toks[-1:] if gf_toks else []
                tcp = p.op("vector", lambda e, ge=ge: e.tensor_copy(
                    self.xn_tm[:, 0:16], ge), deps=[t10] + war)
                pb6, bd6 = self.bwrite(6, deps=[tcp])
                tt = p.op("tensor", lambda e, pb=pb6: e.transpose(
                    pb[0:16, 0:128], self.xn_tm[:, 0:16], self.ident[:, :]),
                    deps=bd6)
                self.bwrote(6, tt)
                _, rd6 = self.bread(6)
                te = p.op("vector", lambda e, pb=pb6, c=c: e.tensor_copy(
                    Gf[0:16, c * 128:(c + 1) * 128], pb[0:16, 0:128]),
                    deps=rd6)
                self.bdidread(6, te)
                gf_toks.append(te)
            self.dump("moeG", Gtm[:, :], gw_toks)
            res_toks = [None] * NTC
            econs_v = {}
            b1cons = []
            tb1 = None
            for v in range(32):
                ei, hh = v // 2, v % 2
                bix = v % 2
                ldep = xnf_toks + econs_v.get(v - 2, [])
                w1t = self.load_weight(w1h[bix],
                                       w1_ap[ei][:, hh * 1024:(hh + 1) * 1024],
                                       8, 1024, deps=ldep, cast_eng="scalar")
                w2t = self.load_weight(w2h[bix],
                                       w2_ap[ei][hh * 1024:(hh + 1) * 1024, :],
                                       8, 1024, deps=ldep, cast_eng="scalar")
                econs = []
                if hh == 0:
                    tb1d = p.dma("gpsimd", b1n[0:16, 0:128],
                                 b1_ap[ei].rearrange("(hc p) -> hc p", p=128),
                                 deps=b1cons + xnf_toks)
                    pb6b, bd6b = self.bwrite(6, deps=[tb1d])
                    tb1t = p.op("tensor", lambda e, pb=pb6b: e.transpose(
                        pb[0:128, 0:16], b1n[0:16, 0:128],
                        self.ident[0:16, 0:16]), deps=bd6b)
                    self.bwrote(6, tb1t)
                    _, rd6b = self.bread(6)
                    tb1 = p.op("vector", lambda e, pb=pb6b: e.tensor_copy(
                        b1t[:, 0:16], pb[0:128, 0:16]), deps=rd6b)
                    self.bdidread(6, tb1)
                    b1cons = [tb1t]
                for s_ in range(4):
                    htoks = []
                    for hc in range(8):
                        bi = hc % 2
                        pb, bd = self.bwrite(bi, deps=w1t + [tb1] + xnf_toks)
                        mm = None
                        for k in range(NDC):
                            mm = p.op("tensor",
                                      lambda e, pb=pb, k=k, hc=hc, s_=s_, bix=bix:
                                      e.matmul(pb[:, 0:512],
                                               lhsT=w1h[bix][:, k * 1024 + hc * 128:
                                                             k * 1024 + hc * 128 + 128],
                                               rhs=self.xn_f[:, k * T + s_ * 512:
                                                             k * T + s_ * 512 + 512],
                                               start=(k == 0), stop=(k == NDC - 1)),
                                      deps=bd if k == 0 else ())
                        self.bwrote(bi, mm)
                        econs.append(mm)
                        _, rd = self.bread(bi)
                        ev = p.op("vector", lambda e, pb=pb, hc=hc, hh=hh:
                                  e.tensor_scalar(
                                      hsup[:, hc * 512:(hc + 1) * 512],
                                      pb[:, 0:512],
                                      b1t[:, hh * 8 + hc: hh * 8 + hc + 1],
                                      0.0, OP.add, OP.max),
                                  deps=rd)
                        self.bdidread(bi, ev)
                        b1cons.append(ev)
                        htoks.append(ev)
                    for tsub in range(4):
                        c = s_ * 4 + tsub
                        for half in range(2):
                            bi = 2 + (tsub * 2 + half) % 2
                            pb, bd = self.bwrite(bi, deps=htoks + w2t)
                            mm = None
                            for hc in range(8):
                                mm = p.op("tensor",
                                          lambda e, pb=pb, hc=hc, tsub=tsub,
                                          half=half, bix=bix:
                                          e.matmul(pb[:, 0:512],
                                                   lhsT=hsup[:, hc * 512 + tsub * 128:
                                                            hc * 512 + tsub * 128 + 128],
                                                   rhs=w2h[bix][:, hc * 1024 + half * 512:
                                                                hc * 1024 + half * 512 + 512],
                                                   start=(hc == 0), stop=(hc == 7)),
                                          deps=bd if hc == 0 else ())
                            self.bwrote(bi, mm)
                            econs.append(mm)
                            _, rd = self.bread(bi)
                            rdep = [res_toks[c]] if res_toks[c] else gw_toks
                            up = p.op("vector", lambda e, pb=pb, c=c, half=half, ei=ei:
                                      e.scalar_tensor_tensor(
                                          self.X[:, c * 1024 + half * 512:
                                                 c * 1024 + half * 512 + 512],
                                          pb[:, 0:512],
                                          Gtm[:, c * 16 + ei: c * 16 + ei + 1],
                                          self.X[:, c * 1024 + half * 512:
                                                 c * 1024 + half * 512 + 512],
                                          OP.mult, OP.add),
                                      deps=rd + rdep)
                            self.bdidread(bi, up)
                            res_toks[c] = up
                econs_v[v] = econs
            self.dump("moeXNF", self.xn_f[:, :], res_toks)
            self.dump("moeXpre", self.X[:, :], res_toks)
            import os as _os2
            if _os2.environ.get("MOE_NOB2"):
                return res_toks
            out = []
            for c in range(NTC):
                for half in range(2):
                    bi = 4 + half
                    pb, bd = self.bwrite(bi, deps=gf_toks + [tb2c, res_toks[c]])
                    mm = p.op("tensor", lambda e, pb=pb, c=c, half=half: e.matmul(
                        pb[:, 0:512], lhsT=Gf[0:16, c * 128: c * 128 + 128],
                        rhs=b2s[0:16, half * 512: half * 512 + 512],
                        start=True, stop=True), deps=bd)
                    self.bwrote(bi, mm)
                    _, rd = self.bread(bi)
                    up = p.op("vector", lambda e, pb=pb, c=c, half=half:
                              e.tensor_tensor(
                                  self.X[:, c * 1024 + half * 512:
                                         c * 1024 + half * 512 + 512],
                                  self.X[:, c * 1024 + half * 512:
                                         c * 1024 + half * 512 + 512],
                                  pb[:, 0:512], OP.add), deps=rd)
                    self.bdidread(bi, up)
                res_toks[c] = up
                out.append(up)
        return out


    def l1_attn(self, wq_ap, wqr_ap, wkr_ap, wkvd_ap, wku_ap, wvu_ap, wo_ap,
                lobo_ap, cap, prev_toks):
        p = self.p
        H, DH, DR = 16, 64, 32
        SCALE = float(np.float32(1.0) / np.sqrt(np.float32(DH + DR)))
        with ExitStack() as sco:
            sbo = lambda n, sh, dt: sco.enter_context(
                self.nc.sbuf_tensor(n, sh, dt))
            self.xn_f = sbo("l1xnf", [128, NDC * T], BF16)
            xnf_toks = self.layernorm(
                lambda c: (self.X[:, c * 1024:(c + 1) * 1024], prev_toks))
            lat = sbo("l1lat", [128, 2 * T], BF16)
            kr = sbo("l1kr", [128, T], BF16)
            lob = sbo("l1lob", [128, 16], F32)
            onesrow = sbo("l1ones", [1, 128], BF16)
            perms = {}
            for nme in ("qcp0", "qcp1", "krp") + tuple(
                    f"qx1_{i}" for i in range(8)) + tuple(
                    f"qx2_{i}" for i in range(8)):
                perms[nme] = sbo("l1" + nme, [128, 96], BF16)
            ptok = [p.dma("sync", perms[nme][:, :], cap["perm_" + nme],
                          deps=prev_toks)
                    for nme in perms]
            t_ones = p.op("vector", lambda e: e.memset(onesrow[0:1, :], 1.0))
            tkz = [p.op("vector", lambda e: e.memset(kr[:, :], 0.0))]
            tlob = p.dma("gpsimd", lob[0:1, 0:16],
                         lobo_ap.rearrange("(o h) -> o h", o=1),
                         deps=prev_toks)
            tlob2 = p.op("scalar", lambda e: e.activation(
                lob[0:1, 0:16], lob[0:1, 0:16], AF.Exp), deps=[tlob])
            tlob3 = p.op("vector", lambda e: e.tensor_copy(
                self.xn_tm[0:1, 0:16], lob[0:1, 0:16]), deps=[tlob2])
            pb, bd = self.bwrite(5, deps=[tlob3, t_ones])
            tlob4 = p.op("tensor", lambda e, pb=pb: e.matmul(
                pb[:, 0:16], lhsT=onesrow[0:1, :],
                rhs=self.xn_tm[0:1, 0:16], start=True, stop=True), deps=bd)
            self.bwrote(5, tlob4)
            _, rd = self.bread(5)
            tlob5 = p.op("vector", lambda e, pb=pb: e.tensor_copy(
                lob[:, 0:16], pb[:, 0:16]), deps=rd)
            self.bdidread(5, tlob5)
            wkvd_b = sbo("l1wkvd", [128, NDC * 256], BF16)
            tw = self.load_weight(wkvd_b, wkvd_ap, NDC, 256, deps=xnf_toks)
            lat_toks = []
            for oc in range(2):
                for tt in range(4):
                    bi = tt % 2
                    pb, bd = self.bwrite(bi, deps=xnf_toks + tw)
                    mm = None
                    for k in range(NDC):
                        mm = p.op("tensor", lambda e, pb=pb, k=k, oc=oc, tt=tt:
                                  e.matmul(pb[:, 0:512],
                                           lhsT=wkvd_b[:, k * 256 + oc * 128:
                                                       k * 256 + oc * 128 + 128],
                                           rhs=self.xn_f[:, k * T + tt * 512:
                                                         k * T + tt * 512 + 512],
                                           start=(k == 0), stop=(k == NDC - 1)),
                                  deps=bd if k == 0 else ())
                    self.bwrote(bi, mm)
                    _, rd = self.bread(bi)
                    ev = p.op("scalar", lambda e, pb=pb, oc=oc, tt=tt: e.copy(
                        lat[:, oc * T + tt * 512: oc * T + tt * 512 + 512],
                        pb[:, 0:512]), deps=rd)
                    self.bdidread(bi, ev)
                    lat_toks.append(ev)
            Qr = sbo("l1QrA", [128, 4 * T], BF16)
            with ExitStack() as sc2:
                sb2 = lambda n, sh, dt: sc2.enter_context(
                    self.nc.sbuf_tensor(n, sh, dt))
                cosb = sb2("l1cos", [128, T], F32)
                sinb = sb2("l1sin", [128, T], F32)
                wkr_b = sb2("l1wkr", [128, NDC * 32], BF16)
                wqr_b = sb2("l1wqr", [128, NDC * 512], BF16)
                rtmp = sb2("l1rt", [128, 512], F32)
                rtmp2 = sb2("l1rt2", [128, 512], F32)
                tcs = [p.dma("sync", cosb[:, :], cap["rope_cos"],
                              deps=prev_toks),
                       p.dma("sync", sinb[:, :], cap["rope_sin"],
                              deps=prev_toks)]
                twr = self.load_weight(wkr_b, wkr_ap, NDC, 32, deps=xnf_toks)
                twq = self.load_weight(wqr_b, wqr_ap, NDC, 512, deps=xnf_toks)
                kr_toks = []
                for tt in range(4):
                    bi = 2 + tt % 2
                    pb, bd = self.bwrite(bi, deps=xnf_toks + twr)
                    mm = None
                    for k in range(NDC):
                        mm = p.op("tensor", lambda e, pb=pb, k=k, tt=tt:
                                  e.matmul(pb[0:16, 0:512],
                                           lhsT=wkr_b[:, k * 32: k * 32 + 16],
                                           rhs=self.xn_f[:, k * T + tt * 512:
                                                         k * T + tt * 512 + 512],
                                           start=(k == 0), stop=(k == NDC - 1)),
                                  deps=bd if k == 0 else ())
                    for k in range(NDC):
                        mm = p.op("tensor", lambda e, pb=pb, k=k, tt=tt:
                                  e.matmul(pb[32:48, 0:512],
                                           lhsT=wkr_b[:, k * 32 + 16:
                                                      k * 32 + 32],
                                           rhs=self.xn_f[:, k * T + tt * 512:
                                                         k * T + tt * 512 + 512],
                                           start=(k == 0), stop=(k == NDC - 1),
                                           tile_position=(0, 32)),
                                  deps=())
                    self.bwrote(bi, mm)
                    _, rd = self.bread(bi, deps=tcs + tkz)
                    cs = cosb[0:16, tt * 512:(tt + 1) * 512]
                    sn = sinb[0:16, tt * 512:(tt + 1) * 512]
                    pbp = self.psum[bi]
                    x1 = pbp[0:16, 0:512]
                    x2 = pbp[32:48, 0:512]
                    a1 = p.op("vector", lambda e, x1=x1, cs=cs: e.tensor_tensor(
                        rtmp[0:16, :], x1, cs, OP.mult), deps=rd)
                    a2 = p.op("vector", lambda e, x2=x2, sn=sn: e.tensor_tensor(
                        rtmp2[0:16, :], x2, sn, OP.mult), deps=rd)
                    a3 = p.op("vector", lambda e, tt=tt: e.tensor_tensor(
                        kr[0:16, tt * 512:(tt + 1) * 512], rtmp[0:16, :],
                        rtmp2[0:16, :], OP.subtract), deps=[a1, a2])
                    b1 = p.op("vector", lambda e, x2=x2, cs=cs: e.tensor_tensor(
                        rtmp[0:16, :], x2, cs, OP.mult), deps=rd + [a3])
                    b2 = p.op("vector", lambda e, x1=x1, sn=sn: e.tensor_tensor(
                        rtmp2[0:16, :], x1, sn, OP.mult), deps=rd + [a3])
                    b3 = p.op("vector", lambda e, tt=tt: e.tensor_tensor(
                        kr[32:48, tt * 512:(tt + 1) * 512], rtmp[0:16, :],
                        rtmp2[0:16, :], OP.add), deps=[b1, b2])
                    for t in (a1, a2, b1, b2):
                        self.bdidread(bi, t)
                    kr_toks.append(b3)
                qr_toks = []
                for ch in range(2):
                    for tt in range(4):
                        pbs = {}
                        for xi in range(2):
                            bi = 2 + xi
                            oc = xi * 2 + ch
                            pb, bd = self.bwrite(bi, deps=xnf_toks + twq)
                            mm = None
                            for k in range(NDC):
                                mm = p.op(
                                    "tensor",
                                    lambda e, pb=pb, k=k, oc=oc, tt=tt:
                                    e.matmul(
                                        pb[:, 0:512],
                                        lhsT=wqr_b[:, k * 512 + oc * 128:
                                                   k * 512 + oc * 128 + 128],
                                        rhs=self.xn_f[:, k * T + tt * 512:
                                                      k * T + tt * 512 + 512],
                                        start=(k == 0), stop=(k == NDC - 1)),
                                    deps=bd if k == 0 else ())
                            self.bwrote(bi, mm)
                            pbs[xi] = bi
                        _, rd2 = self.bread(2, deps=tcs)
                        _, rd3 = self.bread(3, deps=tcs)
                        rd = rd2 + rd3
                        cs = cosb[:, tt * 512:(tt + 1) * 512]
                        sn = sinb[:, tt * 512:(tt + 1) * 512]
                        x1 = self.psum[2][:, 0:512]
                        x2 = self.psum[3][:, 0:512]
                        a1 = p.op("vector", lambda e, x1=x1, cs=cs:
                                  e.tensor_tensor(rtmp[:, :], x1, cs, OP.mult),
                                  deps=rd)
                        a2 = p.op("vector", lambda e, x2=x2, sn=sn:
                                  e.tensor_tensor(rtmp2[:, :], x2, sn, OP.mult),
                                  deps=rd)
                        a3 = p.op("vector", lambda e, ch=ch, tt=tt:
                                  e.tensor_tensor(
                                      Qr[:, ch * T + tt * 512:
                                         ch * T + tt * 512 + 512],
                                      rtmp[:, :], rtmp2[:, :], OP.subtract),
                                  deps=[a1, a2])
                        b1 = p.op("vector", lambda e, x2=x2, cs=cs:
                                  e.tensor_tensor(rtmp[:, :], x2, cs, OP.mult),
                                  deps=rd + [a3])
                        b2 = p.op("vector", lambda e, x1=x1, sn=sn:
                                  e.tensor_tensor(rtmp2[:, :], x1, sn, OP.mult),
                                  deps=rd + [a3])
                        b3 = p.op("vector", lambda e, ch=ch, tt=tt:
                                  e.tensor_tensor(
                                      Qr[:, (2 + ch) * T + tt * 512:
                                         (2 + ch) * T + tt * 512 + 512],
                                      rtmp[:, :], rtmp2[:, :], OP.add),
                                  deps=[b1, b2])
                        for t in (a1, a2, b1, b2):
                            self.bdidread(2, t)
                            self.bdidread(3, t)
                        qr_toks.append(b3)
            res = self._l1_heads(wq_ap, wqr_ap, wku_ap, wvu_ap, wo_ap,
                                 lat, kr, lob, perms, Qr, lat_toks,
                                 kr_toks, qr_toks, xnf_toks, ptok, SCALE)
        return res

    def _l1_heads(self, wq_ap, wqr_ap, wku_ap, wvu_ap, wo_ap, lat, kr, lob,
                  perms, Qr, lat_toks, kr_toks, qr_toks,
                  xnf_toks, ptok, SCALE):
        p = self.p
        res_toks = [None] * NTC
        NHG = 4
        prev_g = []
        for hg in range(NHG):
            with ExitStack() as sch:
                sbh = lambda n, sh, dt: sch.enter_context(
                    self.nc.sbuf_tensor(n + str(hg), sh, dt))
                Qc = sbh("l1Qc", [128, 2 * T], BF16)
                Kc = sbh("l1Kc", [128, 2 * T], BF16)
                Vh = sbh("l1V", [128, NTC * 264], BF16)
                tvones = p.op("vector", lambda e, Vh=Vh: e.memset(
                    Vh[:, :].rearrange("p (n s) -> p n s", s=66)[:, :, 64:65],
                    1.0))
                qc_toks = []
                kc_toks = []
                v_toks = []
                with ExitStack() as scp:
                    sbp = lambda n, sh, dt: scp.enter_context(
                        self.nc.sbuf_tensor(n + str(hg), sh, dt))
                    wb = sbp("l1wq", [128, NDC * 256], BF16)
                    wkub = sbp("l1wku", [128, 2 * 256], BF16)
                    wvub = sbp("l1wvu", [128, 2 * 256], BF16)
                    adeps = qr_toks + kr_toks + prev_g
                    twq = self.load_weight(
                        wb, wq_ap[:, hg * 256:(hg + 1) * 256], NDC, 256,
                        deps=xnf_toks + adeps, cast_eng="scalar")
                    tku = self.load_weight(
                        wkub, wku_ap[:, hg * 256:(hg + 1) * 256], 2, 256,
                        deps=lat_toks + adeps, cast_eng="scalar")
                    tvu = self.load_weight(
                        wvub, wvu_ap[:, hg * 256:(hg + 1) * 256], 2, 256,
                        deps=lat_toks + adeps, cast_eng="scalar")
                    for (dstT, wsrc, nk, src, stoks, wt) in (
                            (Qc, wb, NDC, self.xn_f, qc_toks, twq),
                            (Kc, wkub, 2, lat, kc_toks, tku)):
                        for oc in range(2):
                            for tt in range(4):
                                bi = tt % 2
                                pb, bd = self.bwrite(bi, deps=xnf_toks
                                                     + lat_toks + wt)
                                mm = None
                                for k in range(nk):
                                    mm = p.op(
                                        "tensor",
                                        lambda e, pb=pb, k=k, oc=oc, tt=tt,
                                        wsrc=wsrc, src=src, nk=nk:
                                        e.matmul(
                                            pb[:, 0:512],
                                            lhsT=wsrc[:, k * 256 + oc * 128:
                                                      k * 256 + oc * 128 + 128],
                                            rhs=src[:, k * T + tt * 512:
                                                    k * T + tt * 512 + 512],
                                            start=(k == 0), stop=(k == nk - 1)),
                                        deps=bd if k == 0 else ())
                                self.bwrote(bi, mm)
                                _, rd = self.bread(bi)
                                ev = p.op("scalar",
                                          lambda e, pb=pb, dstT=dstT, oc=oc,
                                          tt=tt: e.copy(
                                              dstT[:, oc * T + tt * 512:
                                                   oc * T + tt * 512 + 512],
                                              pb[:, 0:512]), deps=rd)
                                self.bdidread(bi, ev)
                                stoks.append(ev)
                    for c in range(NTC):
                        bi = 2 + c % 2
                        pb, bd = self.bwrite(bi, deps=lat_toks + tvu)
                        mm = None
                        for k in range(2):
                            mm = p.op("tensor",
                                      lambda e, pb=pb, k=k, c=c, wvub=wvub:
                                      e.matmul(pb[:, 0:256],
                                               lhsT=lat[:, k * T + c * 128:
                                                        k * T + c * 128 + 128],
                                               rhs=wvub[:, k * 256:(k + 1) * 256],
                                               start=(k == 0), stop=(k == 1)),
                                      deps=bd if k == 0 else ())
                        self.bwrote(bi, mm)
                        _, rd = self.bread(bi)
                        for hl4 in range(4):
                            ev = p.op("vector",
                                      lambda e, pb=pb, c=c, Vh=Vh, hl4=hl4:
                                      e.tensor_copy(
                                          Vh[:, c * 264 + hl4 * 66:
                                             c * 264 + hl4 * 66 + 64],
                                          pb[:, hl4 * 64:(hl4 + 1) * 64]),
                                      deps=rd)
                            self.bdidread(bi, ev)
                            v_toks.append(ev)
                        v_toks.append(tvones)
                Qcat = sbh("l1Qcat", [128, T], BF16)
                Kcat = sbh("l1Kcat", [128, T], BF16)
                ET = sbh("l1ET", [128, 512], BF16)
                atm = sbh("l1atm", [128, NTC * 256], BF16)
                Aring = sbh("l1Ar", [128, 2 * 256], BF16)
                wo_b = sbh("l1wo", [128, 2 * 1024], BF16)
                proj_done = qc_toks + kc_toks + v_toks + kr_toks + qr_toks + ptok
                atm_toks = []
                war_cat = []
                for hl in range(4):
                    h = hg * 4 + hl
                    war_cat = self._l1_one_head(
                        h, hl, Qc, Qr, Kc, Vh, Qcat, Kcat, ET, atm, kr, lob,
                        perms, proj_done, war_cat, atm_toks, SCALE)
                two = self.load_weight(
                    wo_b, wo_ap[hg * 256:(hg + 1) * 256, :], 2, 1024,
                    deps=atm_toks, cast_eng="scalar")
                for c in range(NTC):
                    slot = c % 2
                    trt = []
                    for kk in range(2):
                        bi = 6 + kk % 2
                        pb, bd = self.bwrite(bi, deps=atm_toks)
                        tt_ = p.op("tensor", lambda e, pb=pb, c=c, kk=kk, atm=atm:
                                   e.transpose(pb[:, 0:128],
                                               atm[:, c * 256 + kk * 128:
                                                   c * 256 + kk * 128 + 128],
                                               self.ident[:, :]), deps=bd)
                        self.bwrote(bi, tt_)
                        _, rd = self.bread(bi)
                        war_wo = getattr(self, "_l1warwo", {})
                        ev = p.op("vector",
                                  lambda e, pb=pb, slot=slot, kk=kk, Aring=Aring:
                                  e.tensor_copy(
                                      Aring[:, slot * 256 + kk * 128:
                                            slot * 256 + kk * 128 + 128],
                                      pb[:, 0:128]),
                                  deps=rd + ([war_wo[slot]] if slot in war_wo
                                             else []))
                        self.bdidread(bi, ev)
                        trt.append(ev)
                    for half in range(2):
                        bi = half
                        pb, bd = self.bwrite(bi, deps=trt + two)
                        mm = None
                        for kk in range(2):
                            mm = p.op("tensor",
                                      lambda e, pb=pb, kk=kk, half=half,
                                      slot=slot, Aring=Aring, wo_b=wo_b:
                                      e.matmul(
                                          pb[:, 0:512],
                                          lhsT=Aring[:, slot * 256 + kk * 128:
                                                     slot * 256 + kk * 128 + 128],
                                          rhs=wo_b[:, kk * 1024 + half * 512:
                                                   kk * 1024 + half * 512 + 512],
                                          start=(kk == 0), stop=(kk == 1)),
                                      deps=bd if kk == 0 else ())
                        self.bwrote(bi, mm)
                        _, rd = self.bread(bi)
                        up = p.op("vector", lambda e, pb=pb, c=c, half=half:
                                  e.tensor_tensor(
                                      self.X[:, c * 1024 + half * 512:
                                             c * 1024 + half * 512 + 512],
                                      self.X[:, c * 1024 + half * 512:
                                             c * 1024 + half * 512 + 512],
                                      pb[:, 0:512], OP.add),
                                  deps=rd + ([res_toks[c]] if res_toks[c]
                                             else []))
                        self.bdidread(bi, up)
                        wd = getattr(self, "_l1warwo", {})
                        wd[slot] = mm
                        self._l1warwo = wd
                    res_toks[c] = up
                self._l1warwo = {}
                prev_g = [(sn, p.cnt[sn]) for sn in
                          ("tensorc", "vectorc", "scalarc")]
        return res_toks

    def _l1_one_head(self, h, hl, Qc, Qr, Kc, Vh, Qcat, Kcat, ET, atm, kr,
                     lob, perms, proj_done, war_cat, atm_toks, SCALE):
        p = self.p
        cat_toks = []
        for tt in range(4):
            for (dst, plist) in (
                    (Qcat, ((perms["qcp" + str(hl % 2)],
                             Qc[:, (hl // 2) * T + tt * 512:
                                (hl // 2) * T + tt * 512 + 512]),
                            (perms[f"qx1_{h % 8}"],
                             Qr[:, (h // 8) * T + tt * 512:
                                (h // 8) * T + tt * 512 + 512]),
                            (perms[f"qx2_{h % 8}"],
                             Qr[:, (2 + h // 8) * T + tt * 512:
                                (2 + h // 8) * T + tt * 512 + 512]))),
                    (Kcat, ((perms["qcp" + str(hl % 2)],
                             Kc[:, (hl // 2) * T + tt * 512:
                                (hl // 2) * T + tt * 512 + 512]),
                            (perms["krp"],
                             kr[:, tt * 512: tt * 512 + 512])))):
                pb, bd = self.bwrite(4, deps=proj_done + war_cat)
                mm = None
                n = len(plist)
                for i, (pm, src) in enumerate(plist):
                    mm = p.op("tensor",
                              lambda e, pb=pb, pm=pm, src=src, i=i, n=n:
                              e.matmul(pb[0:96, 0:512], lhsT=pm[:, :], rhs=src,
                                       start=(i == 0), stop=(i == n - 1)),
                              deps=bd if i == 0 else ())
                self.bwrote(4, mm)
                _, rd = self.bread(4)
                if dst is Qcat:
                    ev = p.op("scalar", lambda e, pb=pb, dst=dst, tt=tt: e.mul(
                        dst[0:96, tt * 512:(tt + 1) * 512], pb[0:96, 0:512],
                        SCALE), deps=rd)
                else:
                    ev = p.op("scalar", lambda e, pb=pb, dst=dst, tt=tt: e.copy(
                        dst[0:96, tt * 512:(tt + 1) * 512], pb[0:96, 0:512]),
                        deps=rd)
                self.bdidread(4, ev)
                cat_toks.append(ev)
        new_war = list(cat_toks)
        mcols = {}
        warm = {}
        for qt in range(NTC):
            nkt = qt // 4 + 1
            stoks = []
            for kt in range(nkt):
                bi = kt % 3
                pb, bd = self.bwrite(bi, deps=cat_toks)
                mm = p.op("tensor", lambda e, pb=pb, qt=qt, kt=kt: e.matmul(
                    pb[:, 0:512], lhsT=Qcat[0:96, qt * 128:(qt + 1) * 128],
                    rhs=Kcat[0:96, kt * 512:(kt + 1) * 512],
                    start=True, stop=True), deps=bd)
                self.bwrote(bi, mm)
                new_war.append(mm)
                stoks.append((kt, bi, mm))
            dkt, dbi, dmm = stoks[-1]
            _, rd = self.bread(dbi)
            mtoks = []
            for b in range(4):
                kb = dkt * 4 + b
                if kb == qt:
                    tmk = p.op("vector", lambda e, pb=self.psum[dbi], b=b:
                               e.tensor_tensor(pb[:, b * 128:(b + 1) * 128],
                                               pb[:, b * 128:(b + 1) * 128],
                                               self.trilneg[:, :], OP.add),
                               deps=rd)
                    mtoks.append(tmk)
                elif kb > qt:
                    tmk = p.op("vector", lambda e, pb=self.psum[dbi], b=b:
                               e.memset(pb[:, b * 128:(b + 1) * 128], -1e30),
                               deps=rd)
                    mtoks.append(tmk)
            for t in mtoks:
                self.bdidread(dbi, t)
            c0 = (qt % 4) * 32
            m = self.stat[:, c0: c0 + 1]
            negm = self.stat[:, c0 + 4: c0 + 5]
            mx2 = self.stat[:, c0 + 8: c0 + 9]
            r = self.stat[:, c0 + 12: c0 + 13]
            mcols[qt] = (m, negm, r)
            mprev = warm.get(qt % 4)
            tm_ = None
            for i, (kt, bi, mm) in enumerate(stoks):
                _, rd = self.bread(bi, deps=(mtoks if kt == dkt else []))
                dst = m if i == 0 else mx2
                tm_ = p.op("vector", lambda e, pb=self.psum[bi], dst=dst:
                           e.tensor_reduce(dst, pb[:, 0:512], AX, OP.max),
                           deps=rd + ([mprev] if (mprev and i == 0) else [])
                           + ([tm_] if i else []))
                self.bdidread(bi, tm_)
                if i:
                    tm_ = p.op("vector", lambda e, m=m, mx2=mx2:
                               e.tensor_scalar(m, mx2, m, None, OP.max),
                               deps=[tm_])
            tneg = p.op("vector", lambda e, m=m, negm=negm: e.tensor_scalar(
                negm, m, -1.0, None, OP.mult), deps=[tm_])
            mcols[("done", qt)] = tneg
            warm[qt % 4] = tneg
        for qt5 in range(4):
            av_mm = {}
            et_war = None
            for kb in range(qt5 * 4 + 4):
                pb3, bd3 = self.bwrite(4, deps=cat_toks)
                etmm = p.op("tensor", lambda e, pb=pb3, qt5=qt5, kb=kb: e.matmul(
                    pb[:, 0:512], lhsT=Kcat[0:96, kb * 128:(kb + 1) * 128],
                    rhs=Qcat[0:96, qt5 * 512:(qt5 + 1) * 512],
                    start=True, stop=True), deps=bd3)
                self.bwrote(4, etmm)
                new_war.append(etmm)
                _, rd3 = self.bread(4)
                etev = p.op("scalar", lambda e, pb=pb3: e.activation(
                    ET[:, 0:512], pb[:, 0:512], AF.Exp),
                    deps=rd3 + ([et_war] if et_war else []))
                self.bdidread(4, etev)
                masked = [etev]
                for b in range(4):
                    qb = qt5 * 4 + b
                    if kb == qb:
                        tmk = p.op("vector", lambda e, b=b: e.tensor_tensor(
                            ET[:, b * 128:(b + 1) * 128],
                            ET[:, b * 128:(b + 1) * 128],
                            self.triu[:, :], OP.mult), deps=[etev])
                        masked.append(tmk)
                    elif kb > qb:
                        tmk = p.op("vector", lambda e, b=b: e.memset(
                            ET[:, b * 128:(b + 1) * 128], 0.0), deps=[etev])
                        masked.append(tmk)
                last_mm = None
                for b in range(4):
                    qb = qt5 * 4 + b
                    if kb > qb:
                        continue
                    pb5, bd5 = self.bwrite(b, deps=masked)
                    last_mm = p.op("tensor",
                                   lambda e, pb=pb5, b=b, kb=kb, qb=qb, hl=hl,
                                   Vh=Vh: e.matmul(
                                       pb[:, 0:65],
                                       lhsT=ET[:, b * 128:(b + 1) * 128],
                                       rhs=Vh[:, kb * 264 + hl * 66:
                                              kb * 264 + hl * 66 + 65],
                                       start=(kb == 0), stop=(kb == qb)),
                                   deps=bd5)
                    self.bwrote(b, last_mm)
                    av_mm[qb] = last_mm
                et_war = last_mm if last_mm is not None else etev
            for b in range(4):
                qb = qt5 * 4 + b
                m, negm, r = mcols[qb]
                tneg = mcols[("done", qb)]
                _, rd5 = self.bread(b)
                tR = p.op("vector", lambda e, pb=self.psum[b], b=b, h=h, r=r:
                          e.tensor_scalar(r, pb[:, 64:65],
                                          lob[:, h:h + 1], None, OP.add),
                          deps=rd5 + [av_mm[qb], tneg])
                self.bdidread(b, tR)
                tR2 = p.op("vector", lambda e, r=r: e.reciprocal(r, r),
                           deps=[tR])
                em = self.stat[:, (qb % 4) * 32 + 16: (qb % 4) * 32 + 17]
                tE = p.op("scalar", lambda e, negm=negm, em=em: e.activation(
                    em, negm, AF.Exp), deps=[tneg])
                tR3 = p.op("vector", lambda e, r=r, em=em: e.tensor_tensor(
                    r, r, em, OP.mult), deps=[tR2, tE])
                _, rd5b = self.bread(b)
                ev = p.op("vector", lambda e, pb=self.psum[b], b=b, qb=qb,
                          hl=hl, r=r, atm=atm: e.tensor_scalar(
                              atm[:, qb * 256 + hl * 64: qb * 256 + hl * 64 + 64],
                              pb[:, 0:64], r, None, OP.mult),
                          deps=rd5b + [tR3])
                self.bdidread(b, ev)
                atm_toks.append(ev)
        return new_war


    def _wquant_unit(self, hbm_ap, r0, nrow, ncol, swb, wnat, wq_t, deps):
        p = self.p
        toks = []
        nrc = nrow // 128
        prev4 = None
        thr = self.ln_scr2[:, 9:10]
        nthr = self.ln_scr2[:, 10:11]
        r1 = p.op("vector", lambda e: e.reciprocal(
            self.ln_scr2[:, 8:9], swb), deps=list(deps))
        r2 = p.op("vector", lambda e: e.tensor_scalar(
            thr, self.ln_scr2[:, 8:9], 0.5, None, OP.mult), deps=[r1])
        r3 = p.op("vector", lambda e: e.tensor_scalar(
            nthr, thr, -1.0, None, OP.mult), deps=[r2])
        for k in range(nrc):
            for c0 in range(0, ncol, 512):
                w = min(512, ncol - c0)
                i = self.wring_i % 2
                self.wring_i += 1
                slot = self.wring[:, i * 512: i * 512 + w]
                d = list(deps)
                if self.wring_rel[i] is not None:
                    d.append(self.wring_rel[i])
                td = p.dma("sync", slot,
                           hbm_ap[r0 + k * 128: r0 + (k + 1) * 128,
                                  c0:c0 + w], deps=d)
                t1 = p.op("vector", lambda e, slot=slot, w=w: e.tensor_scalar(
                    self.ffq32[:, 0:w], slot, thr, None, OP.is_ge),
                    deps=[td, r3] + ([prev4] if prev4 else []))
                t2 = p.op("vector", lambda e, slot=slot, w=w: e.tensor_scalar(
                    self.ffq32[:, 512:512 + w], slot, nthr, None, OP.is_le),
                    deps=[td, r3] + ([prev4] if prev4 else []))
                t4 = p.op("vector", lambda e, k=k, c0=c0, w=w, wnat=wnat:
                          e.tensor_tensor(wnat[:, k * ncol + c0:
                                               k * ncol + c0 + w],
                                          self.ffq32[:, 0:w],
                                          self.ffq32[:, 512:512 + w],
                                          OP.subtract), deps=[t1, t2])
                prev4 = t4
                self.wring_rel[i] = t2
                toks.append(t4)
        out = []
        for k in range(nrc):
            for j in range(ncol // 128):
                bi = 6 + (j % 2)
                pb, bd = self.bwrite(bi, deps=toks)
                tt_ = p.op("tensor", lambda e, pb=pb, k=k, j=j, wnat=wnat:
                           e.transpose(pb[:, 0:128],
                                       wnat[:, k * ncol + j * 128:
                                            k * ncol + j * 128 + 128],
                                       self.ident[:, :]), deps=bd)
                self.bwrote(bi, tt_)
                _, rd = self.bread(bi)
                ev = p.op("vector", lambda e, pb=pb, k=k, j=j, wq_t=wq_t:
                          e.tensor_copy(wq_t[:, j * nrow + k * 128:
                                             j * nrow + (k + 1) * 128],
                                        pb[:, 0:128]), deps=rd)
                self.bdidread(bi, ev)
                out.append(ev)
        return out

    def _bcast_cols_to_row(self, cols_ap_fn, n, rowbuf, deps):
        p = self.p
        out = []
        for c in range(n):
            src, d = cols_ap_fn(c)
            bi = c % 2
            pb, bd = self.bwrite(bi, deps=d + deps)
            tt_ = p.op("tensor", lambda e, pb=pb, src=src: e.transpose(
                pb[0:1, 0:128], src, self.identf[:, :]), deps=bd)
            self.bwrote(bi, tt_)
            _, rd = self.bread(bi)
            ev = p.op("vector", lambda e, pb=pb, c=c, rowbuf=rowbuf:
                      e.tensor_copy(rowbuf[0:1, c * 128:(c + 1) * 128],
                                    pb[0:1, 0:128]), deps=rd)
            self.bdidread(bi, ev)
            out.append(ev)
        return out

    def _bcast_row_to_tiles(self, rowbuf, dst_bf, n512, deps):
        p = self.p
        out = []
        for s in range(n512):
            bi = 2 + s % 2
            pb, bd = self.bwrite(bi, deps=deps)
            mm = p.op("tensor", lambda e, pb=pb, s=s, rowbuf=rowbuf: e.matmul(
                pb[:, 0:512], lhsT=self.onesrowf[0:1, :],
                rhs=rowbuf[0:1, s * 512:(s + 1) * 512],
                start=True, stop=True), deps=bd)
            self.bwrote(bi, mm)
            _, rd = self.bread(bi)
            ev = p.op("vector", lambda e, pb=pb, s=s, dst_bf=dst_bf:
                      e.tensor_copy(dst_bf[:, s * 512:(s + 1) * 512],
                                    pb[:, 0:512]), deps=rd)
            self.bdidread(bi, ev)
            out.append(ev)
        return out

    def _wabs_sum(self, hbm_ap, kchunks, cols, acc_col, deps):
        p = self.p
        t0 = p.op("vector", lambda e, acc_col=acc_col: e.memset(acc_col, 0.0),
                  deps=deps)
        prev = t0
        part = self.stat[:, 240:241]
        for k in range(kchunks):
            for c0 in range(0, cols, 512):
                w = min(512, cols - c0)
                i = self.wring_i % 2
                self.wring_i += 1
                slot = self.wring[:, i * 512: i * 512 + w]
                d = list(deps)
                if self.wring_rel[i] is not None:
                    d.append(self.wring_rel[i])
                td = p.dma("sync", slot, hbm_ap[k * 128:(k + 1) * 128,
                                                c0:c0 + w], deps=d)
                ta = p.op("scalar", lambda e, slot=slot, w=w: e.activation(
                    self.ln_scr[:, 0:w], slot, AF.Abs, accum_out=part),
                    deps=[td, prev])
                tb = p.op("vector", lambda e, acc_col=acc_col:
                          e.tensor_tensor(acc_col, acc_col, part, OP.add),
                          deps=[ta])
                self.wring_rel[i] = ta
                prev = tb
        return prev

    def _scalar_reduce_bcast(self, acc_col, n_elem, sw_dst, fac_dst, deps):
        p = self.p
        hi = self.ffbf[:, 0:1]
        lo = self.ffbf[:, 1:2]
        t1 = p.op("vector", lambda e: e.tensor_copy(hi, acc_col), deps=deps)
        t2 = p.op("vector", lambda e: e.scalar_tensor_tensor(
            self.ffq32[:, 0:1], hi, -1.0, acc_col, OP.mult, OP.add),
            deps=[t1])
        t3 = p.op("vector", lambda e: e.tensor_copy(lo, self.ffq32[:, 0:1]),
                  deps=[t2])
        pb, bd = self.bwrite(0, deps=[t3])
        mm = p.op("tensor", lambda e, pb=pb: e.matmul(
            pb[0:1, 0:1], lhsT=hi, rhs=self.onescol[:, 0:1],
            start=True, stop=False), deps=bd)
        mm = p.op("tensor", lambda e, pb=pb: e.matmul(
            pb[0:1, 0:1], lhsT=lo, rhs=self.onescol[:, 0:1],
            start=False, stop=True), deps=())
        self.bwrote(0, mm)
        _, rd = self.bread(0)
        t4 = p.op("vector", lambda e, pb=pb, n_elem=n_elem: e.tensor_scalar(
            self.ffq32[0:1, 0:1], pb[0:1, 0:1], 1.0 / n_elem, 1e-5,
            OP.mult, OP.max), deps=rd)
        self.bdidread(0, t4)
        t5 = p.op("vector", lambda e: e.reciprocal(
            self.ffq32[0:1, 1:2], self.ffq32[0:1, 0:1]), deps=[t4])
        t6 = p.op("vector", lambda e: e.tensor_scalar(
            self.ffq32[0:1, 2:3], self.ffq32[0:1, 0:1], 1.0 / 127.0, None,
            OP.mult), deps=[t4])
        outs = []
        for (src_i, dst) in ((1, sw_dst), (2, fac_dst)):
            th = p.op("vector", lambda e, src_i=src_i: e.tensor_copy(
                self.ffbf[0:1, 2:3], self.ffq32[0:1, src_i:src_i + 1]),
                deps=[t5, t6])
            tl0 = p.op("vector", lambda e, src_i=src_i: e.scalar_tensor_tensor(
                self.ffq32[0:1, 3:4], self.ffbf[0:1, 2:3], -1.0,
                self.ffq32[0:1, src_i:src_i + 1], OP.mult, OP.add), deps=[th])
            tl = p.op("vector", lambda e: e.tensor_copy(
                self.ffbf[0:1, 3:4], self.ffq32[0:1, 3:4]), deps=[tl0])
            pb, bd = self.bwrite(1, deps=[tl])
            mm = p.op("tensor", lambda e, pb=pb: e.matmul(
                pb[:, 0:1], lhsT=self.onesrow[0:1, :],
                rhs=self.ffbf[0:1, 2:3], start=True, stop=False), deps=bd)
            mm = p.op("tensor", lambda e, pb=pb: e.matmul(
                pb[:, 0:1], lhsT=self.onesrow[0:1, :],
                rhs=self.ffbf[0:1, 3:4], start=False, stop=True), deps=())
            self.bwrote(1, mm)
            _, rd = self.bread(1)
            ev = p.op("vector", lambda e, pb=pb, dst=dst: e.tensor_copy(
                dst, pb[:, 0:1]), deps=rd)
            self.bdidread(1, ev)
            outs.append(ev)
        return outs


    def l1_ffn(self, w1_ap, cw_ap, cb_ap, w2_ap, prev_toks):
        p = self.p
        with ExitStack() as sco:
            sbo = lambda n, sh, dt: sco.enter_context(
                self.nc.sbuf_tensor(n, sh, dt))
            self.ffq32 = sbo("ffq32", [128, 1024], F32)
            self.ffi32 = sbo("ffi32", [128, 1024], mybir.dt.int32)
            self.ffbf = sbo("ffbf", [128, 4], BF16)
            acc = sbo("ffacc", [128, 8], F32)
            swf = sbo("ffswf", [128, 4], F32)
            rowb = sbo("ffrow", [1, 128], F32)
            t1 = self._wabs_sum(w1_ap, 32, 1024, acc[:, 0:1], prev_toks)
            t2 = self._wabs_sum(w2_ap, 8, 4096, acc[:, 4:5], [t1])
            sc1 = self._scalar_reduce_bcast(acc[:, 0:1], 4096 * 1024,
                                            swf[:, 0:1], swf[:, 1:2], [t1])
            sc2 = self._scalar_reduce_bcast(acc[:, 4:5], 4096 * 1024,
                                            swf[:, 2:3], swf[:, 3:4], [t2])
            cwt = sbo("ffcwt", [128, 32 * 4], F32)
            cw32 = sbo("ffcw32", [128, 512], F32)
            tda = p.dma("sync", cw32[0:32, 0:384],
                        cw_ap.rearrange("(cc p) t -> cc (p t)", p=128),
                        deps=[t2])
            tdb = p.dma("sync", cw32[0:32, 384:512],
                        cb_ap.rearrange("(cc p) -> cc p", p=128), deps=[t2])
            ctoks = []
            cview = cw32[:, 0:384].rearrange("q (pp t) -> q pp t", t=3)
            for ti in range(4):
                bi = ti % 2
                pb, bd = self.bwrite(bi, deps=[tda, tdb])
                if ti < 3:
                    src = cview[0:32, :, ti]
                else:
                    src = cw32[0:32, 384:512]
                tt_ = p.op("tensor", lambda e, pb=pb, src=src: e.transpose(
                    pb[:, 0:32], src, self.identf[0:32, 0:32]), deps=bd)
                self.bwrote(bi, tt_)
                _, rd = self.bread(bi)
                dv = cwt[:, :].rearrange("p (cc f) -> p cc f", f=4)
                ev = p.op("vector", lambda e, pb=pb, dv=dv, ti=ti:
                          e.tensor_copy(dv[:, :, ti], pb[:, 0:32]), deps=rd)
                self.bdidread(bi, ev)
                ctoks.append(ev)
            xq_f = sbo("ffxqf", [128, NDC * T], BF16)
            qstat = sbo("ffqst", [128, 4 * NTC], F32)
            xqr = sbo("ffxqr", [128, 2 * 1024], BF16)
            self._qsink_prev = None
            ab = sbo("ffab", [128, T], BF16)
            s2b = sbo("ffs2b", [128, T], BF16)
            m2c = sbo("ffm2", [128, NTC], F32)
            xqf_toks = []
            ring_rel = {}

            def qsink(c, dst, tn):
                col = lambda j: qstat[:, c * 4 + j: c * 4 + j + 1]
                m1, s1, al = col(0), col(1), col(2)
                t1_ = p.op("vector", lambda e, dst=dst, m1=m1: e.tensor_reduce(
                    m1, dst, AX, OP.max, apply_absolute_value=True), deps=[tn])
                t2_ = p.op("vector", lambda e, m1=m1: e.tensor_scalar(
                    m1, m1, 1e-5, None, OP.max), deps=[t1_])
                t3_ = p.op("vector", lambda e, s1=s1, m1=m1: e.reciprocal(
                    s1, m1), deps=[t2_])
                t4_ = p.op("vector", lambda e, s1=s1: e.tensor_scalar(
                    s1, s1, 127.0, None, OP.mult), deps=[t3_])
                ta = p.op("vector", lambda e, al=al, m1=m1: e.tensor_tensor(
                    al, m1, swf[:, 1:2], OP.mult), deps=[t2_] + sc1)
                slot = c % 2
                xqs = xqr[:, slot * 1024:(slot + 1) * 1024]
                dd = [t4_] + ([ring_rel[slot]] if slot in ring_rel else []) \
                    + ([self._qsink_prev] if self._qsink_prev else [])
                t5_ = p.op("vector", lambda e, dst=dst, s1=s1: e.tensor_scalar(
                    self.ffq32[:, 0:1024], dst, s1, 128.0, OP.mult, OP.add),
                    deps=dd)
                t6_ = p.op("vector", lambda e: e.tensor_copy(
                    self.ffi32[:, 0:1024], self.ffq32[:, 0:1024]), deps=[t5_])
                t6b = p.op("vector", lambda e: e.tensor_tensor(
                    self.ffq32[:, 0:1024], self.ffq32[:, 0:1024],
                    self.ffi32[:, 0:1024], OP.subtract), deps=[t6_])
                t6c = p.op("vector", lambda e: e.scalar_tensor_tensor(
                    self.ffq32[:, 0:1024], self.ffq32[:, 0:1024], 0.5,
                    self.ffi32[:, 0:1024], OP.is_ge, OP.add), deps=[t6b])
                t7_ = p.op("vector", lambda e, xqs=xqs: e.tensor_scalar(
                    xqs, self.ffq32[:, 0:1024], -128.0, None, OP.add),
                    deps=[t6c])
                toks = [ta]
                for k in range(NDC):
                    bi = 6 + (k % 2)
                    pb, bd = self.bwrite(bi, deps=[t7_])
                    tt_ = p.op("tensor", lambda e, pb=pb, xqs=xqs, k=k:
                               e.transpose(pb[:, 0:128],
                                           xqs[:, k * 128:(k + 1) * 128],
                                           self.ident[:, :]), deps=bd)
                    self.bwrote(bi, tt_)
                    _, rd = self.bread(bi)
                    ev = p.op("vector", lambda e, pb=pb, c=c, k=k:
                              e.tensor_copy(
                                  xq_f[:, k * T + c * 128: k * T + (c + 1) * 128],
                                  pb[:, 0:128]), deps=rd)
                    self.bdidread(bi, ev)
                    toks.append(ev)
                self._qsink_prev = t7_
                ring_rel[slot] = toks[-1]
                xqf_toks.extend(toks)
                return toks[-1], toks

            lt = self.layernorm(
                lambda c: (self.X[:, c * 1024:(c + 1) * 1024], prev_toks),
                sink=qsink)
            abt2 = []
            for c in range(NTC):
                abt2.append(self._col_to_btile(
                    qstat[:, c * 4 + 2: c * 4 + 3],
                    ab[:, c * 128:(c + 1) * 128], rowb, lt))
            self.dump("ffswf", swf[:, :], sc1 + sc2)
            self.dump("ffab", ab[:, :], abt2)
            self.dump("ffqstat0", qstat[:, :], lt + abt2)
            self.dump("ffcwt", cwt[:, :], ctoks)
            self.dump("ffxqf", xq_f[:, :], xqf_toks)
            res = self._ffn_passes(w1_ap, w2_ap, cwt, swf, xq_f, ab, s2b,
                                   qstat, m2c, rowb,
                                   xqf_toks + sc1 + sc2 + ctoks + abt2, lt)
            self.dump("ffm2", m2c[:, :], res)
            self.dump("ffs2b", s2b[:, :], res)
            self.dump("ffqstat", qstat[:, :], res)
        return res

    def _col_to_btile(self, col_ap, dst_slice, rowb, deps):
        p = self.p
        pb, bd = self.bwrite(2, deps=deps)
        tt_ = p.op("tensor", lambda e, pb=pb, col_ap=col_ap: e.transpose(
            pb[0:1, 0:128], col_ap, self.identf[:, :]), deps=bd)
        self.bwrote(2, tt_)
        _, rd = self.bread(2)
        ev1 = p.op("vector", lambda e, pb=pb: e.tensor_copy(
            rowb[0:1, 0:128], pb[0:1, 0:128]), deps=rd)
        self.bdidread(2, ev1)
        pb3, bd3 = self.bwrite(3, deps=[ev1])
        mm = p.op("tensor", lambda e, pb=pb3: e.matmul(
            pb[:, 0:128], lhsT=self.onesrowf[0:1, :], rhs=rowb[0:1, 0:128],
            start=True, stop=True), deps=bd3)
        self.bwrote(3, mm)
        _, rd3 = self.bread(3)
        ev2 = p.op("vector", lambda e, pb=pb3, dst_slice=dst_slice:
                   e.tensor_copy(dst_slice, pb[:, 0:128]), deps=rd3)
        self.bdidread(3, ev2)
        return ev2


    def _ffn_passes(self, w1_ap, w2_ap, cwt, swf, xq_f, ab, s2b, qstat, m2c,
                    rowb, deps0, lt):
        p = self.p
        res_toks = [None] * NTC
        with ExitStack() as scope:
            sbl = lambda n, sh, dt: scope.enter_context(
                self.nc.sbuf_tensor(n, sh, dt))
            w1nat = sbl("ffw1n", [128, 4 * 1024], BF16)
            w1qu = sbl("ffw1q", [128, 8 * 512], BF16)
            w2nat = sbl("ffw2n", [128, 8 * 512], BF16)
            w2qu = sbl("ffw2q", [128, 4 * 1024], BF16)
            rr = sbl("ffrr", [128, 2 * 4 * 512], BF16)
            xq2 = sbl("ffxq2", [128, 2 * 4 * 512], BF16)
            h2s = sbl("ffh2s", [128, 512], BF16)
            tmp = sbl("fftmp", [128, 512], F32)
            carry = sbl("ffcar", [128, 8], F32)
            tmz = p.op("vector", lambda e: e.memset(m2c[:, :], 0.0))
            prevu = deps0 + [tmz] + lt
            ringwar = {}
            xq2war = {}

            def consume(u, s1, pi, slot_toks, wt2):
                cons = []
                chain = slot_toks[s1][-1:]
                for h4 in range(4):
                    cc = u * 4 + h4
                    sl = s1 % 2
                    rsl = rr[:, sl * 2048 + h4 * 512: sl * 2048 + h4 * 512 + 512]
                    cw0 = cwt[:, cc * 4: cc * 4 + 1]
                    cw1 = cwt[:, cc * 4 + 1: cc * 4 + 2]
                    cw2 = cwt[:, cc * 4 + 2: cc * 4 + 3]
                    ccb = cwt[:, cc * 4 + 3: cc * 4 + 4]
                    d0 = slot_toks[s1] + chain
                    t1 = p.op("vector", lambda e, rsl=rsl, cw1=cw1, ccb=ccb:
                              e.tensor_scalar(tmp[:, 0:512], rsl, cw1, ccb,
                                              OP.mult, OP.add), deps=d0)
                    t2 = p.op("vector", lambda e, rsl=rsl, cw0=cw0:
                              e.scalar_tensor_tensor(
                                  tmp[:, 1:512], rsl[:, 0:511], cw0,
                                  tmp[:, 1:512], OP.mult, OP.add), deps=[t1])
                    if s1 > 0:
                        t2 = p.op("vector", lambda e, h4=h4, cw0=cw0:
                                  e.scalar_tensor_tensor(
                                      tmp[:, 0:1], carry[:, h4:h4 + 1], cw0,
                                      tmp[:, 0:1], OP.mult, OP.add), deps=[t2])
                    t3 = p.op("vector", lambda e, rsl=rsl, cw2=cw2:
                              e.scalar_tensor_tensor(
                                  tmp[:, 0:511], rsl[:, 1:512], cw2,
                                  tmp[:, 0:511], OP.mult, OP.add), deps=[t2])
                    if s1 < 3:
                        nsl = (s1 + 1) % 2
                        nfc = rr[:, nsl * 2048 + h4 * 512:
                                 nsl * 2048 + h4 * 512 + 1]
                        t3 = p.op("vector", lambda e, nfc=nfc, cw2=cw2:
                                  e.scalar_tensor_tensor(
                                      tmp[:, 511:512], nfc, cw2,
                                      tmp[:, 511:512], OP.mult, OP.add),
                                  deps=[t3] + slot_toks[s1 + 1][h4:h4 + 1])
                    tc_ = p.op("vector", lambda e, rsl=rsl, h4=h4:
                               e.tensor_copy(carry[:, h4:h4 + 1],
                                             rsl[:, 511:512]), deps=[t3])
                    ts_ = p.op("scalar", lambda e: e.activation(
                        h2s[:, 0:512], tmp[:, 0:512], AF.Silu), deps=[t3])
                    if pi == 0 and u == 0 and s1 == 0 and h4 == 0:
                        self.dump("ffh2s00", h2s[:, 0:512], [ts_])
                        self.dump("ffrr00", rr[:, 0:512], [ts_])
                    if pi == 0:
                        mt = ts_
                        for tb in range(4):
                            c = s1 * 4 + tb
                            bi = 6 + tb % 2
                            pb, bd = self.bwrite(bi, deps=[ts_])
                            tt_ = p.op("tensor", lambda e, pb=pb, tb=tb:
                                       e.transpose(pb[:, 0:128],
                                                   h2s[:, tb * 128:
                                                       (tb + 1) * 128],
                                                   self.ident[:, :]), deps=bd)
                            self.bwrote(bi, tt_)
                            _, rd = self.bread(bi)
                            mp = self.ffq32[:, 512:513]
                            r1 = p.op("vector", lambda e, pb=pb, mp=mp:
                                      e.tensor_reduce(
                                          mp, pb[:, 0:128], AX, OP.max,
                                          apply_absolute_value=True),
                                      deps=rd + [mt])
                            self.bdidread(bi, r1)
                            r2 = p.op("vector", lambda e, c=c, mp=mp:
                                      e.tensor_scalar(
                                          m2c[:, c:c + 1], mp,
                                          m2c[:, c:c + 1], None, OP.max),
                                      deps=[r1])
                            mt = r2
                        cons.append(mt)
                        chain = [mt]
                    else:
                        sl2 = s1 % 2
                        xdst = xq2[:, sl2 * 2048 + h4 * 512:
                                   sl2 * 2048 + h4 * 512 + 512]
                        qd = [ts_] + xq2war.get((sl2, h4), [])
                        q1 = p.op("vector", lambda e, s1=s1: e.tensor_tensor(
                            self.ffq32[:, 0:512], h2s[:, 0:512],
                            s2b[:, s1 * 512:(s1 + 1) * 512], OP.mult), deps=qd)
                        q2 = p.op("vector", lambda e: e.tensor_scalar(
                            self.ffq32[:, 0:512], self.ffq32[:, 0:512],
                            128.0, None, OP.add), deps=[q1])
                        q3 = p.op("vector", lambda e: e.tensor_copy(
                            self.ffi32[:, 0:512], self.ffq32[:, 0:512]),
                            deps=[q2])
                        q3b = p.op("vector", lambda e: e.tensor_tensor(
                            self.ffq32[:, 0:512], self.ffq32[:, 0:512],
                            self.ffi32[:, 0:512], OP.subtract), deps=[q3])
                        q3c = p.op("vector", lambda e: e.scalar_tensor_tensor(
                            self.ffq32[:, 0:512], self.ffq32[:, 0:512], 0.5,
                            self.ffi32[:, 0:512], OP.is_ge, OP.add),
                            deps=[q3b])
                        q4 = p.op("vector", lambda e, xdst=xdst:
                                  e.tensor_scalar(xdst, self.ffq32[:, 0:512],
                                                  -128.0, None, OP.add),
                                  deps=[q3c])
                        cons.append(q4)
                        chain = [q4]
                if pi == 1:
                    sl2 = s1 % 2
                    for t128 in range(4):
                        c = s1 * 4 + t128
                        for dh in range(2):
                            bi = 2 + (t128 * 2 + dh) % 2
                            pb, bd = self.bwrite(bi, deps=cons + wt2)
                            mm = None
                            for h4 in range(4):
                                mm = p.op(
                                    "tensor",
                                    lambda e, pb=pb, h4=h4, t128=t128, dh=dh,
                                    sl2=sl2: e.matmul(
                                        pb[:, 0:512],
                                        lhsT=xq2[:, sl2 * 2048 + h4 * 512
                                                 + t128 * 128:
                                                 sl2 * 2048 + h4 * 512
                                                 + t128 * 128 + 128],
                                        rhs=w2qu[:, h4 * 1024 + dh * 512:
                                                 h4 * 1024 + dh * 512 + 512],
                                        start=(h4 == 0), stop=(h4 == 3)),
                                    deps=bd if h4 == 0 else ())
                            self.bwrote(bi, mm)
                            for h4 in range(4):
                                xq2war.setdefault((sl2, h4), []).append(mm)
                            _, rd = self.bread(bi)
                            up = p.op("vector",
                                      lambda e, pb=pb, c=c, dh=dh: e.scalar_tensor_tensor(
                                          self.X[:, c * 1024 + dh * 512:
                                                 c * 1024 + dh * 512 + 512],
                                          pb[:, 0:512],
                                          qstat[:, c * 4 + 3: c * 4 + 4],
                                          self.X[:, c * 1024 + dh * 512:
                                                 c * 1024 + dh * 512 + 512],
                                          OP.mult, OP.add),
                                      deps=rd + ([res_toks[c]]
                                                 if res_toks[c] else []))
                            self.bdidread(bi, up)
                            res_toks[c] = up
                return cons

            for pi in (0, 1):
                units = list(range(8)) if pi == 0 else list(range(7, -1, -1))
                for u in units:
                    if not (pi == 1 and u == 7):
                        wt1 = self._wquant_unit(w1_ap, u * 512, 512, 1024,
                                                swf[:, 0:1], w1nat, w1qu,
                                                deps=prevu + [])
                    else:
                        wt1 = []
                    wt2 = []
                    if pi == 1:
                        wt2 = self._wquant_unit(
                            w2_ap[:, u * 512:(u + 1) * 512], 0, 1024, 512,
                            swf[:, 2:3], w2nat, w2qu, deps=prevu)
                    if pi == 0 and u == 0:
                        self.dump("ffw1q0", w1qu[:, :], wt1)
                        self.dump("ffw1n0", w1nat[:, :], wt1)
                    slot_toks = {}
                    for s_ in range(4):
                        sl = s_ % 2
                        htoks = []
                        for h4 in range(4):
                            bi = h4 % 2
                            pb, bd = self.bwrite(bi, deps=wt1 + prevu)
                            mm = None
                            for k in range(NDC):
                                mm = p.op(
                                    "tensor",
                                    lambda e, pb=pb, k=k, h4=h4, s_=s_:
                                    e.matmul(
                                        pb[:, 0:512],
                                        lhsT=w1qu[:, k * 512 + h4 * 128:
                                                  k * 512 + h4 * 128 + 128],
                                        rhs=xq_f[:, k * T + s_ * 512:
                                                 k * T + s_ * 512 + 512],
                                        start=(k == 0), stop=(k == NDC - 1)),
                                    deps=bd if k == 0 else ())
                            self.bwrote(bi, mm)
                            _, rd = self.bread(bi)
                            ev = p.op("vector",
                                      lambda e, pb=pb, sl=sl, h4=h4, s_=s_:
                                      e.tensor_tensor(
                                          rr[:, sl * 2048 + h4 * 512:
                                             sl * 2048 + h4 * 512 + 512],
                                          pb[:, 0:512],
                                          ab[:, s_ * 512:(s_ + 1) * 512],
                                          OP.mult),
                                      deps=rd + ringwar.get((sl, h4), []))
                            self.bdidread(bi, ev)
                            htoks.append(ev)
                        slot_toks[s_] = htoks
                        if s_ > 0:
                            cons = consume(u, s_ - 1, pi, slot_toks, wt2)
                            for h4 in range(4):
                                ringwar[((s_ - 1) % 2, h4)] = cons
                    cons = consume(u, 3, pi, slot_toks, wt2)
                    for h4 in range(4):
                        ringwar[(1, h4)] = cons
                    prevu = [(sn, p.cnt[sn]) for sn in
                             ("tensorc", "vectorc", "scalarc")]
                if pi == 0:
                    sts = []
                    for c in range(NTC):
                        mp = m2c[:, c:c + 1]
                        b1 = p.op("vector", lambda e, mp=mp: e.tensor_scalar(
                            mp, mp, 1e-5, None, OP.max), deps=prevu)
                        b2 = p.op("vector", lambda e, c=c, mp=mp:
                                  e.tensor_tensor(
                                      qstat[:, c * 4 + 3: c * 4 + 4], mp,
                                      swf[:, 3:4], OP.mult), deps=[b1])
                        b3 = p.op("vector", lambda e, c=c, mp=mp:
                                  e.reciprocal(
                                      self.ffq32[:, 513:514], mp), deps=[b2])
                        b4 = p.op("vector", lambda e, c=c: e.tensor_scalar(
                            self.ffq32[:, 513:514], self.ffq32[:, 513:514],
                            127.0, None, OP.mult), deps=[b3])
                        b5 = self._col_to_btile(
                            self.ffq32[:, 513:514],
                            s2b[:, c * 128:(c + 1) * 128], rowb, [b4])
                        sts.append(b5)
                    prevu = prevu + sts
        return res_toks



def _wqr_perm():
    order = []
    for xi in range(2):
        for h in range(16):
            for i in range(16):
                order.append(h * 32 + xi * 16 + i)
    return np.array(order)


def kernel(**inputs):
    nc = bass.Bass("TRN2", target_bir_lowering=False,
                   detect_race_conditions=False,
                   dynamic_dma_scratch_size=8192)
    x = nc.dram_tensor("x", [T, D], F32, kind="ExternalInput").ap()

    def dt_(n, sh):
        return nc.dram_tensor(n, sh, F32, kind="ExternalInput").ap()
    a_wq = dt_("a_wq", [D, D]); a_wk = dt_("a_wk", [D, D])
    a_wv = dt_("a_wv", [D, D]); a_wo = dt_("a_wo", [D, D])
    m_g = dt_("m_g", [D, 16]); m_w1 = dt_("m_w1", [16, D, 2048])
    m_b1 = dt_("m_b1", [16, 2048]); m_w2 = dt_("m_w2", [16, 2048, D])
    m_b2 = dt_("m_b2", [16, D])
    q_wq = dt_("q_wq", [D, 1024]); q_wqr = dt_("q_wqr", [D, 512])
    q_wkr = dt_("q_wkr", [D, 32]); q_wkvd = dt_("q_wkvd", [D, 256])
    q_wku = dt_("q_wku", [256, 1024]); q_wvu = dt_("q_wvu", [256, 1024])
    q_wo = dt_("q_wo", [D, D]); q_lobo = dt_("q_lobo", [16])
    f_w1 = dt_("f_w1", [4096, D]); f_cw = dt_("f_cw", [4096, 3])
    f_cb = dt_("f_cb", [4096]); f_w2 = dt_("f_w2", [D, 4096])

    consts = host_constants()
    BF = ml_bf16()
    cap = {}
    for n, a in consts.items():
        if n.startswith("dmask"):
            continue
        dt = BF16 if a.dtype == BF else F32
        cap[n] = nc.dram_tensor("c_" + n, list(a.shape), dt,
                                kind="ExternalInput").ap()
    o_x = nc.dram_tensor("o_x", [T, D], F32, kind="ExternalOutput").ap()

    with ExitStack() as ctx:
        p = Prog(nc, ctx)
        b = Builder(nc, ctx, p)
        tc = b.load_consts(cap)
        tx = b.load_X(x)
        t1 = b.l0_attn(a_wq, a_wk, a_wv, a_wo, [tx] + tc)
        t2 = b.l0_ffn(m_g, m_w1, m_b1, m_w2, m_b2, t1)
        t3 = b.l1_attn(q_wq, q_wqr, q_wkr, q_wkvd, q_wku, q_wvu, q_wo,
                       q_lobo, cap, t2)
        t4 = b.l1_ffn(f_w1, f_cw, f_cb, f_w2, t3)
        st = b.store_X(o_x, t4)
        block = ctx.enter_context(nc.Block())
        p.finalize(block, [st])

    xfull = np.ascontiguousarray(np.asarray(inputs["L0_attn_x"],
                                            dtype=np.float32))
    shared = dict(
        a_wq=inputs["L0_attn_W_q"], a_wk=inputs["L0_attn_W_k"],
        a_wv=inputs["L0_attn_W_v"], a_wo=inputs["L0_attn_W_o"],
        m_g=inputs["L0_ffn_gate"], m_w1=inputs["L0_ffn_W1"],
        m_b1=inputs["L0_ffn_b1"], m_w2=inputs["L0_ffn_W2"],
        m_b2=inputs["L0_ffn_b2"],
        q_wq=inputs["L1_attn_Wq"],
        q_wqr=np.ascontiguousarray(
            np.asarray(inputs["L1_attn_Wqr"],
                       dtype=np.float32)[:, _wqr_perm()]),
        q_wkr=inputs["L1_attn_Wkr"], q_wkvd=inputs["L1_attn_Wkvd"],
        q_wku=inputs["L1_attn_Wku"], q_wvu=inputs["L1_attn_Wvu"],
        q_wo=inputs["L1_attn_Wo"], q_lobo=inputs["L1_attn_lobo_log"],
        f_w1=inputs["L1_ffn_w1"],
        f_cw=np.ascontiguousarray(
            np.asarray(inputs["L1_ffn_conv_w"],
                       dtype=np.float32).reshape(4096, 3)),
        f_cb=inputs["L1_ffn_conv_b"], f_w2=inputs["L1_ffn_w2"],
    )
    shared = {k: np.ascontiguousarray(np.asarray(v, dtype=np.float32))
              for k, v in shared.items()}
    for n, a in consts.items():
        if n.startswith("dmask"):
            continue
        shared["c_" + n] = np.ascontiguousarray(np.asarray(a))

    maps = []
    for core in range(8):
        im = dict(shared)
        im["x"] = xfull[core]
        maps.append(im)
    res = run_bass_kernel_spmd(nc, maps, core_ids=list(range(8)))
    out = np.stack([np.asarray(res.results[c]["o_x"], dtype=np.float32)
                    for c in range(8)], axis=0)
    return out
